# version 16
# baseline (speedup 1.0000x reference)
import numpy as np
from contextlib import ExitStack

import concourse.bass as bass
import concourse.mybir as mybir
from concourse.bass_utils import run_bass_kernel_spmd

F32 = mybir.dt.float32
BF16 = mybir.dt.bfloat16
FP16 = mybir.dt.float16
FP8 = mybir.dt.float8e4
AF = mybir.ActivationFunctionType
ALU = mybir.AluOpType

NCORES = 8
T = 2048
D = 1024
NTT = 16
NSC = 4
NC = 8
NH = 16
MH32 = (63.0 / 32.0) * 1.02



class Buf:
    __slots__ = ("name", "w", "r")

    def __init__(self, name):
        self.name = name
        self.w = None
        self.r = []


_NQ = 8


class Stream:
    def __init__(self, name, is_dma):
        self.name = name
        self.is_dma = is_dma
        self.items = []


class KB:
    def __init__(self, nc):
        self.nc = nc
        self.st = {n: Stream(n, n in ("gp", "sy"))
                   for n in ("te", "ve", "se", "gp", "sy")}
        self._bar = None
        self._bar_done = set()

    def op(self, sname, fn, reads=(), writes=(), tr=False):
        st = self.st[sname]
        seq = len(st.items)
        deps = set()
        if self._bar is not None and sname not in self._bar_done:
            for bn, bseq in self._bar.items():
                if bseq >= 0:
                    deps.add((self.st[bn], bseq))
            self._bar_done.add(sname)
            if len(self._bar_done) == len(self.st):
                self._bar = None
        for b in reads:
            if b.w is not None:
                deps.add(b.w)
        for b in writes:
            if b.w is not None:
                deps.add(b.w)
            deps.update(b.r)
        deps = {d for d in deps if not (d[0] is st and d[1] >= seq)}
        if sname == "te":
            deps = {d for d in deps if d[0] is not st}
        dmax = {}
        for (pst, pseq) in deps:
            key = (pst, pseq % 8) if pst.is_dma else pst
            if dmax.get(key, (None, -1))[1] < pseq:
                dmax[key] = (pst, pseq)
        deps = set(dmax.values())
        st.items.append([fn, deps, st.is_dma, tr])
        pos = (st, seq)
        for b in reads:
            b.r.append(pos)
        for b in writes:
            b.w = pos
            b.r = []
        return pos

    def te(self, fn, reads=(), writes=(), tr=False):
        return self.op("te", fn, reads, writes, tr)

    def ve(self, fn, reads=(), writes=()):
        return self.op("ve", fn, reads, writes)

    def se(self, fn, reads=(), writes=()):
        return self.op("se", fn, reads, writes)

    def gp(self, fn, reads=(), writes=()):
        return self.op("gp", fn, reads, writes)

    def sy(self, fn, reads=(), writes=()):
        return self.op("sy", fn, reads, writes)

    def barrier(self):
        self._bar = {n: len(s.items) - 1 for n, s in self.st.items()}
        self._bar_done = set()

    def finalize(self):
        nc = self.nc

        waitplans = {n: [] for n in self.st}
        know = {}
        K = {n: {} for n in self.st}
        ptr = {n: 0 for n in self.st}
        done = {n: -1 for n in self.st}
        prog = True
        while prog:
            prog = False
            for n, st in self.st.items():
                while ptr[n] < len(st.items):
                    it = st.items[ptr[n]]
                    if not all(done[pst.name] >= pseq for (pst, pseq) in it[1]):
                        break
                    k = ptr[n]
                    Ks = K[n]
                    if st.is_dma and k >= _NQ:
                        key = (n, (k - _NQ) % _NQ)
                        if Ks.get(key, -1) < k - _NQ:
                            Ks[key] = k - _NQ
                        kn = know.get((n, k - _NQ))
                        if kn:
                            for kk, vv in kn.items():
                                if Ks.get(kk, -1) < vv:
                                    Ks[kk] = vv
                    deps = sorted(it[1], key=lambda d: (d[0] is st, d[0].name, d[1]))
                    if (n == "te" and k > 0 and
                            (it[3] or st.items[k - 1][3])):
                        deps = deps + [(st, k - 1)]
                    waits = []
                    for (pst, pseq) in deps:
                        key = (pst.name, pseq % _NQ) if pst.is_dma else pst.name
                        if Ks.get(key, -1) >= pseq:
                            continue
                        waits.append((pst, pseq))
                        Ks[key] = pseq
                        kn = know.get((pst.name, pseq))
                        if kn:
                            for kk, vv in kn.items():
                                if Ks.get(kk, -1) < vv:
                                    Ks[kk] = vv
                    snap = dict(Ks)
                    skey = (n, k % _NQ) if st.is_dma else n
                    if snap.get(skey, -1) < k:
                        snap[skey] = k
                    know[(n, k)] = snap
                    waitplans[n].append(waits)
                    done[n] = k
                    ptr[n] += 1
                    prog = True
        if any(ptr[n] < len(self.st[n].items) for n in self.st):
            raise RuntimeError("waitplan scheduling stuck")
        self.waitplans = waitplans

        for st in self.st.values():
            for it in st.items:
                it[2] = st.is_dma
        for n in self.st:
            for waits in waitplans[n]:
                for (pst, pseq) in waits:
                    pst.items[pseq][2] = True
        for st in self.st.values():
            cnt = 0
            st.inc_at = []
            for it in st.items:
                if it[2]:
                    cnt += 16 if st.is_dma else 1
                st.inc_at.append(cnt)
            st.total = cnt
        ptr = {n: 0 for n in self.st}
        done = {n: -1 for n in self.st}
        prog = True
        while prog:
            prog = False
            for n, st in self.st.items():
                while ptr[n] < len(st.items):
                    if all(done[pst.name] >= pseq for (pst, pseq) in st.items[ptr[n]][1]):
                        done[n] = ptr[n]
                        ptr[n] += 1
                        prog = True
                    else:
                        break
        stuck = [n for n in self.st if ptr[n] < len(self.st[n].items)]
        if stuck:
            msg = []
            for n in stuck:
                it = self.st[n].items[ptr[n]]
                unmet = [(pst.name, pseq) for (pst, pseq) in it[1] if done[pst.name] < pseq]
                msg.append(f"stream {n} stuck at {ptr[n]} waiting {unmet}")
            raise RuntimeError("DEADLOCK:\n" + "\n".join(msg))

        NQ = 8
        es = ExitStack()
        sems = {}
        for n, st in self.st.items():
            if st.is_dma:
                sems[n] = [es.enter_context(nc.semaphore(f"sem_{n}{q}")) for q in range(NQ)]
            else:
                sems[n] = es.enter_context(nc.semaphore(f"sem_{n}"))
        kbself = self

        def dep_target(pst, pseq):
            if pst.is_dma:
                return (sems[pst.name][pseq % NQ], 16 * (pseq // NQ + 1))
            return (sems[pst.name], pst.inc_at[pseq])

        def emit_stream(st, eng):
            plans = kbself.waitplans[st.name]
            for k, it in enumerate(st.items):
                if st.is_dma and k >= NQ:
                    sem, val = dep_target(st, k - NQ)
                    eng.wait_ge(sem, val)
                for (pst, pseq) in plans[k]:
                    sem, val = dep_target(pst, pseq)
                    if val > 0:
                        eng.wait_ge(sem, val)
                inst = it[0](eng)
                if it[2]:
                    if st.is_dma:
                        inst.then_inc(sems[st.name][k % NQ], 16)
                    else:
                        inst.then_inc(sems[st.name], 1)
            if st.is_dma:
                n = len(st.items)
                for q in range(NQ):
                    cnt = 16 * ((n - q + NQ - 1) // NQ)
                    if cnt > 0:
                        eng.wait_ge(sems[st.name][q], cnt)

        with es, nc.Block() as block:
            @block.tensor
            def _(eng):
                emit_stream(kbself.st["te"], eng)

            @block.vector
            def _(eng):
                emit_stream(kbself.st["ve"], eng)

            @block.scalar
            def _(eng):
                emit_stream(kbself.st["se"], eng)

            @block.gpsimd
            def _(eng):
                emit_stream(kbself.st["gp"], eng)

            @block.sync
            def _(eng):
                emit_stream(kbself.st["sy"], eng)



def colblocks(A):
    K, M = A.shape
    return np.ascontiguousarray(
        A.reshape(K // 128, 128, M // 128, 128).transpose(2, 1, 0, 3).reshape(M // 128, 128, K))


def make_causal_masks():
    p = np.arange(128)[:, None]
    j = np.arange(512)[None, :]
    m = np.zeros((128, 4, 512), dtype=np.float32)
    for k in range(4):
        m[:, k, :] = np.where(j < 128 * k + p, np.float32(-1e9), np.float32(0.0))
    return np.ascontiguousarray(m.reshape(128, 2048))


def host_layout(inputs):
    x = np.asarray(inputs["L0_attn_x"], dtype=np.float32)
    hyp = np.asarray(inputs["L0_attn_hyperplanes"], dtype=np.float32)
    rm = np.asarray(inputs["L1_ffn_rail_maker"], dtype=np.float32)
    ru = np.asarray(inputs["L1_ffn_rail_up_proj"], dtype=np.float32)
    shared = {
        "wqT": colblocks(np.ascontiguousarray(np.asarray(inputs["L0_attn_Wq"], np.float32).T)),
        "wvT": colblocks(np.ascontiguousarray(np.asarray(inputs["L0_attn_Wv"], np.float32).T)),
        "wfT": colblocks(np.ascontiguousarray(np.asarray(inputs["L0_ffn_weight"], np.float32).T)),
        "wk1": colblocks(np.asarray(inputs["L1_attn_Wk"], np.float32)),
        "wq1": colblocks(np.asarray(inputs["L1_attn_Wq"], np.float32)),
        "wv1": colblocks(np.asarray(inputs["L1_attn_Wv"], np.float32)),
        "wcT": colblocks(np.ascontiguousarray(np.asarray(inputs["L1_ffn_w_cproj"], np.float32).T)),
        "hyp2": np.ascontiguousarray(np.concatenate([hyp[:64], hyp[:64]], axis=0)),
        "hypb": np.ascontiguousarray(hyp[64:65].T),
        "pow2": np.ascontiguousarray((2.0 ** np.arange(6)).astype(np.float32)[:, None]),
        "bqc": np.ascontiguousarray(np.asarray(inputs["L0_attn_bq"], np.float32).reshape(8, 128).T),
        "bvc": np.ascontiguousarray(np.asarray(inputs["L0_attn_bv"], np.float32).reshape(8, 128).T),
        "bfc": np.ascontiguousarray(np.asarray(inputs["L0_ffn_bias"], np.float32).reshape(8, 128).T),
        "identD": np.eye(128, dtype=np.float32),
        "cmaskD": make_causal_masks(),
        "rmTs": np.ascontiguousarray(rm.transpose(3, 0, 1, 2).reshape(16, 4096)),
        "ruTs": np.ascontiguousarray(ru.transpose(3, 0, 1, 2).reshape(16, 16384)),
    }
    maps = []
    for b in range(NCORES):
        m = dict(shared)
        m["xT"] = np.ascontiguousarray(x[b].T)
        maps.append(m)
    return maps



class Ctx:
    pass


def build_program(cfg=None):
    cfg = dict(cfg or {})
    phases = cfg.get("phases", 6)
    taps = set(cfg.get("taps", ()))

    nc = bass.Bass("TRN2", target_bir_lowering=False)
    kb = KB(nc)
    c = Ctx()
    c.nc, c.kb, c.taps = nc, kb, taps
    c.tap_names = []

    def din(name, shape):
        return nc.dram_tensor(name, shape, F32, kind="ExternalInput").ap()

    c.xT = din("xT", [D, T])
    c.wqT = din("wqT", [8, 128, 1024]); c.wvT = din("wvT", [8, 128, 1024])
    c.wfT = din("wfT", [8, 128, 1024])
    c.wk1 = din("wk1", [8, 128, 1024]); c.wq1 = din("wq1", [8, 128, 1024])
    c.wv1 = din("wv1", [8, 128, 1024])
    c.wcT = din("wcT", [8, 128, 16384])
    c.hyp2D = din("hyp2", [128, 6]); c.hypbD = din("hypb", [6, 1]); c.pow2D = din("pow2", [6, 1])
    c.bqcD = din("bqc", [128, 8]); c.bvcD = din("bvc", [128, 8]); c.bfcD = din("bfc", [128, 8])
    c.identD = din("identD", [128, 128]); c.cmaskD = din("cmaskD", [128, 2048])
    c.rmTsD = din("rmTs", [16, 4096]); c.ruTsD = din("ruTs", [16, 16384])

    c.outT = nc.dram_tensor("outT", [D, T], F32, kind="ExternalOutput").ap()
    c.b_outT = Buf("outT")
    c.xspill = nc.dram_tensor("xspill", [D, T], F32, kind="Internal").ap()
    c.b_xspill = Buf("xspill")
    c.ffn1sp = nc.dram_tensor("ffn1sp", [D, T], F32, kind="Internal").ap()
    c.b_ffn1sp = Buf("ffn1sp")
    c.fpart = nc.dram_tensor("fpart", [D, T], F32, kind="Internal").ap()
    c.b_fpart = Buf("fpart")
    c.drows = nc.dram_tensor("drows", [64, T], F32, kind="Internal").ap()
    c.b_drows = Buf("drows")
    c.lnrows = nc.dram_tensor("lnrows", [8, T], F32, kind="Internal").ap()
    c.b_lnrows = Buf("lnrows")
    c.rows16 = nc.dram_tensor("rows16", [4, T], BF16, kind="Internal").ap()
    c.b_rows16 = Buf("rows16")

    def tap(name, ap_src, reads, shape, dtype=F32):
        if name not in taps:
            return
        t = nc.dram_tensor("tap_" + name, shape, dtype, kind="ExternalOutput").ap()
        c.tap_names.append("tap_" + name)
        kb.sy(lambda e, t=t, a=ap_src: e.dma_start(out=t, in_=a), reads=list(reads))
    c.tap = tap

    top = ExitStack()
    c.top = top

    def sb(name, shape, dtype, stack=top):
        h = stack.enter_context(nc.sbuf_tensor(name, shape, dtype))
        return h, Buf(name)

    def pp(name, shape, dtype, stack=top):
        h = stack.enter_context(nc.psum_tensor(name, shape, dtype))
        return h, Buf(name)

    c.sb, c.pp = sb, pp

    c.XN = [sb(f"XN{i}", [128, T], BF16) for i in range(NC)]
    c.BIGA = [sb(f"BA{i}", [128, T], BF16) for i in range(NC)]
    c.rbA = sb("rbA", [1, T], F32)
    c.mrow = sb("mrow", [1, 16], F32)
    c.msc = sb("msc", [1, 16], F32)
    c.sca = sb("sca", [1, 4], F32)
    c.ones1b = sb("ones1b", [1, 128], BF16)
    c.ones1f = sb("ones1f", [1, 128], F32)
    c.ones128 = sb("ones128", [128, 1], BF16)
    c.onesFsq = sb("onesFsq", [128, 128], F32)
    c.halfones = sb("halfones", [128, 2], BF16)
    c.identB = sb("identB", [128, 128], BF16)
    c.identF = sb("identF", [128, 128], F32)
    c.hyp2 = sb("hyp2S", [128, 6], BF16)
    c.hypb = sb("hypbS", [6, 1], F32)
    c.pow2 = sb("pow2S", [6, 1], BF16)
    c.bqc = sb("bqcS", [128, 8], F32)
    c.bvc = sb("bvcS", [128, 8], F32)
    c.bfc = sb("bfcS", [128, 8], F32)
    c.bucketT = sb("bucketT", [128, 256], F32)
    c.fcst = sb("fcst", [128, 10], F32)
    _CONSTVALS = [0.0, 1.0, 1e-5, 1.0 / 32.0, -0.05, -0.03, 0.1111111111111111,
                  9.223372036854776e+18,
                  62.0 / 288.0]
    for k, v in enumerate(_CONSTVALS):
        nc.const_aps.register(F32, v, c.fcst[0][:, k:k + 1])
    c.constvals = _CONSTVALS

    c.bigT = [top.enter_context(nc.psum_tensor(f"BG{i}", [128, 1024], F32))
              for i in range(2)]
    c.ps = []
    for i in range(4):
        hb = c.bigT[i // 2]
        c.ps.append((hb[:, 512 * (i % 2):512 * (i % 2) + 512], Buf(f"PS{i}")))
    c.ps += [pp(f"PS{i}", [128, 512], F32) for i in range(4, 7)]
    c.pT = pp("PT", [128, 1024], BF16)

    c.xstack = ExitStack()
    c.X = [sb(f"X{i}", [128, T], F32, c.xstack) for i in range(NC)]

    _init_consts(c)
    _load_x(c)

    if phases >= 1:
        _p1_ln(c)
    if phases >= 2:
        _p2_ffn_qv(c)
    if phases >= 3:
        _p3_l0_attn(c, nh=cfg.get("p3_nh", NH))
    if phases >= 4:
        _p4_ln2_spill(c)
    if phases >= 5:
        _p5_l1_ffn(c)
    if phases >= 6:
        _p6_l1_attn(c)

    kb.finalize()
    for sname in ("vtstack", "xnqstack", "xstack"):
        s = getattr(c, sname, None)
        if s is not None:
            s.close()
    top.close()
    return nc, c


def _init_consts(c):
    kb = c.kb
    kb.ve(lambda e: e.memset(c.ones1b[0][:], 1.0), writes=[c.ones1b[1]])
    kb.ve(lambda e: e.memset(c.ones1f[0][:], 1.0), writes=[c.ones1f[1]])
    kb.ve(lambda e: e.memset(c.ones128[0][:], 1.0), writes=[c.ones128[1]])
    kb.ve(lambda e: e.memset(c.onesFsq[0][:], 1.0), writes=[c.onesFsq[1]])
    kb.ve(lambda e: e.memset(c.halfones[0][:], 0.0), writes=[c.halfones[1]])
    kb.ve(lambda e: e.memset(c.halfones[0][0:64, 0:1], 1.0), writes=[c.halfones[1]])
    kb.ve(lambda e: e.memset(c.halfones[0][64:128, 1:2], 1.0), writes=[c.halfones[1]])
    kb.gp(lambda e: e.dma_start(out=c.identB[0][:], in_=c.identD), writes=[c.identB[1]])
    kb.sy(lambda e: e.dma_start(out=c.identF[0][:], in_=c.identD), writes=[c.identF[1]])
    kb.gp(lambda e: e.dma_start(out=c.hyp2[0][:], in_=c.hyp2D), writes=[c.hyp2[1]])
    kb.sy(lambda e: e.dma_start(out=c.hypb[0][:], in_=c.hypbD), writes=[c.hypb[1]])
    kb.gp(lambda e: e.dma_start(out=c.pow2[0][:], in_=c.pow2D), writes=[c.pow2[1]])
    kb.sy(lambda e: e.dma_start(out=c.bqc[0][:], in_=c.bqcD), writes=[c.bqc[1]])
    kb.sy(lambda e: e.dma_start(out=c.bvc[0][:], in_=c.bvcD), writes=[c.bvc[1]])
    kb.sy(lambda e: e.dma_start(out=c.bfc[0][:], in_=c.bfcD), writes=[c.bfc[1]])
    for k, v in enumerate(c.constvals):
        kb.ve(lambda e, k=k, v=v: e.memset(c.fcst[0][:, k:k + 1], v), writes=[c.fcst[1]])
    kb.barrier()


def _load_x(c):
    kb = c.kb
    for i in range(NC):
        h, b = c.X[i]
        kb.sy(lambda e, h=h, i=i: e.dma_start(out=h[:], in_=c.xT[128 * i:128 * (i + 1), :]),
              writes=[b])



_LN_N = [0]


def _ln(c, dst, fp8_dst, stack):
    kb, sb, ps = c.kb, c.sb, c.ps
    st = stack
    _LN_N[0] += 1
    tag = f"ln{_LN_N[0]}"
    sqb = sb(tag + "_sq", [128, 512], BF16, st)
    mean = sb(tag + "_mean", [1, T], F32, st)
    msq = sb(tag + "_msq", [1, T], F32, st)
    m2 = sb(tag + "_m2", [1, T], F32, st)
    tb = [sb(f"{tag}_tb{i}", [128, 512], F32, st) for i in range(2)]

    for i in range(NC):
        kb.ve(lambda e, i=i: e.tensor_copy(out=dst[i][0][:], in_=c.X[i][0][:]),
              reads=[c.X[i][1]], writes=[dst[i][1]])
    for s in range(NSC):
        for i in range(NC):
            kb.te(lambda e, i=i, s=s: e.matmul(
                ps[0][0][0:1, :], c.ones128[0][:], dst[i][0][:, 512 * s:512 * (s + 1)],
                start=(i == 0), stop=(i == NC - 1)),
                reads=[dst[i][1], c.ones128[1]], writes=[ps[0][1]])
        kb.ve(lambda e, s=s: e.tensor_scalar(
            out=mean[0][0:1, 512 * s:512 * (s + 1)], in0=ps[0][0][0:1, :],
            scalar1=1.0 / 1024.0, scalar2=None, op0=ALU.mult),
            reads=[ps[0][1]], writes=[mean[1]])
    for s in range(NSC):
        for i in range(NC):
            kb.se(lambda e, i=i, s=s: e.activation(
                out=sqb[0][:], in_=c.X[i][0][:, 512 * s:512 * (s + 1)], func=AF.Square),
                reads=[c.X[i][1]], writes=[sqb[1]])
            kb.te(lambda e, i=i, s=s: e.matmul(
                ps[1][0][0:1, :], c.ones128[0][:], sqb[0][:],
                start=(i == 0), stop=(i == NC - 1)),
                reads=[sqb[1], c.ones128[1]], writes=[ps[1][1]])
        kb.ve(lambda e, s=s: e.tensor_scalar(
            out=msq[0][0:1, 512 * s:512 * (s + 1)], in0=ps[1][0][0:1, :],
            scalar1=1.0 / 1024.0, scalar2=None, op0=ALU.mult),
            reads=[ps[1][1]], writes=[msq[1]])
    kb.ve(lambda e: e.tensor_mul(m2[0][:], mean[0][:], mean[0][:]),
          reads=[mean[1]], writes=[m2[1]])
    kb.ve(lambda e: e.tensor_sub(m2[0][:], msq[0][:], m2[0][:]),
          reads=[msq[1], m2[1]], writes=[m2[1]])
    kb.se(lambda e: e.activation(out=m2[0][:], in_=m2[0][:], func=AF.Sqrt, bias=1e-5),
          reads=[m2[1]], writes=[m2[1]])
    kb.ve(lambda e: e.reciprocal(out=msq[0][:], in_=m2[0][:]),
          reads=[m2[1]], writes=[msq[1]])
    kb.ve(lambda e: e.tensor_mul(m2[0][:], mean[0][:], msq[0][:]),
          reads=[mean[1], msq[1]], writes=[m2[1]])
    for s in range(NSC):
        pr = ps[2 * (s % 2)]
        pm = ps[2 * (s % 2) + 1]
        kb.te(lambda e, s=s, pr=pr: e.matmul(
            pr[0][:], c.ones1f[0][:], msq[0][0:1, 512 * s:512 * (s + 1)],
            start=True, stop=True), reads=[msq[1], c.ones1f[1]], writes=[pr[1]])
        kb.te(lambda e, s=s, pm=pm: e.matmul(
            pm[0][:], c.ones1f[0][:], m2[0][0:1, 512 * s:512 * (s + 1)],
            start=True, stop=True), reads=[m2[1], c.ones1f[1]], writes=[pm[1]])
        for i in range(NC):
            tbi = tb[i % 2]
            kb.ve(lambda e, i=i, s=s, tbi=tbi, pr=pr: e.tensor_mul(
                tbi[0][:], c.X[i][0][:, 512 * s:512 * (s + 1)], pr[0][:]),
                reads=[c.X[i][1], pr[1]], writes=[tbi[1]])
            kb.ve(lambda e, i=i, s=s, tbi=tbi, pm=pm: e.tensor_sub(
                dst[i][0][:, 512 * s:512 * (s + 1)], tbi[0][:], pm[0][:]),
                reads=[tbi[1], pm[1]], writes=[dst[i][1]])
            if fp8_dst is not None:
                kb.ve(lambda e, i=i, s=s, tbi=tbi, pm=pm: e.scalar_tensor_tensor(
                    out=fp8_dst[i][0][:, 512 * s:512 * (s + 1)], in0=tbi[0][:],
                    scalar=0.0, in1=pm[0][:], op0=ALU.add, op1=ALU.subtract),
                    reads=[tbi[1], pm[1]], writes=[fp8_dst[i][1]])


def _p1_ln(c):
    c.xnqstack = ExitStack()
    c.XNQ = [c.sb(f"XQ{i}", [128, T], FP8, c.xnqstack) for i in range(NC)]
    st = ExitStack()
    _ln(c, c.XN, c.XNQ, st)
    c.tap("xn0", c.XN[0][0][:], [c.XN[0][1]], [128, T], BF16)
    c.tap("xq0", c.XNQ[0][0][:], [c.XNQ[0][1]], [128, T], FP8)
    st.close()
    c.kb.barrier()


def _p4_ln2_spill(c):
    kb = c.kb
    st = ExitStack()
    _ln(c, c.XN, None, st)
    c.tap("xn1", c.XN[0][0][:], [c.XN[0][1]], [128, T], BF16)
    for i in range(NC):
        kb.sy(lambda e, i=i: e.dma_start(
            out=c.xspill[128 * i:128 * (i + 1), :], in_=c.X[i][0][:]),
            reads=[c.X[i][1]], writes=[c.b_xspill])
    st.close()
    c.xstack.close()
    kb.barrier()



def _p2_ffn_qv(c):
    kb, sb, ps = c.kb, c.sb, c.ps
    st = ExitStack()

    xq16 = c.BIGA
    for i in range(NC):
        kb.ve(lambda e, i=i: e.tensor_copy(out=xq16[i][0][:], in_=c.XNQ[i][0][:]),
              reads=[c.XNQ[i][1]], writes=[xq16[i][1]])
    ffst = ExitStack()
    wf32 = [sb(f"wf32_{i}", [128, 1024], F32, ffst) for i in range(2)]
    wf8 = [sb(f"wf8_{i}", [128, 1024], FP8, ffst) for i in range(2)]
    wf16 = [sb(f"wf16_{i}", [128, 1024], BF16, ffst) for i in range(2)]
    fbuf = [sb(f"fbuf{i}", [128, 512], F32, ffst) for i in range(2)]
    for e_ in range(NC):
        sl = e_ % 2
        kb.sy(lambda e, e_=e_, sl=sl: e.dma_start(out=wf32[sl][0][:], in_=c.wfT[e_]),
              writes=[wf32[sl][1]])
        kb.ve(lambda e, sl=sl: e.tensor_scalar(
            out=wf8[sl][0][:], in0=wf32[sl][0][:], scalar1=448.0, scalar2=-448.0,
            op0=ALU.min, op1=ALU.max), reads=[wf32[sl][1]], writes=[wf8[sl][1]])
        kb.ve(lambda e, sl=sl: e.tensor_copy(out=wf16[sl][0][:], in_=wf8[sl][0][:]),
              reads=[wf8[sl][1]], writes=[wf16[sl][1]])
        for s in range(NSC):
            pb = ps[s % 2]
            for k in range(NC):
                kb.te(lambda e, s=s, k=k, sl=sl, pb=pb: e.matmul(
                    pb[0][:], wf16[sl][0][:, 128 * k:128 * (k + 1)],
                    xq16[k][0][:, 512 * s:512 * (s + 1)],
                    start=(k == 0), stop=(k == NC - 1)),
                    reads=[wf16[sl][1], xq16[k][1]], writes=[pb[1]])
            fb = fbuf[s % 2]
            kb.ve(lambda e, e_=e_, pb=pb, fb=fb: e.tensor_scalar(
                out=fb[0][:], in0=pb[0][:], scalar1=c.bfc[0][:, e_:e_ + 1],
                scalar2=None, op0=ALU.add),
                reads=[pb[1], c.bfc[1]], writes=[fb[1]])
            kb.ve(lambda e, e_=e_, s=s, fb=fb: e.tensor_add(
                c.X[e_][0][:, 512 * s:512 * (s + 1)],
                c.X[e_][0][:, 512 * s:512 * (s + 1)], fb[0][:]),
                reads=[fb[1], c.X[e_][1]], writes=[c.X[e_][1]])
    ffst.close()
    c.xnqstack.close()
    kb.barrier()
    c.tap("xffn0", c.X[0][0][:], [c.X[0][1]], [128, T])

    c.vtstack = ExitStack()
    c.vtok = [sb(f"VT{r}", [128, 1040], BF16, c.vtstack) for r in range(NTT)]
    for r in range(NTT):
        for h in range(NH):
            kb.ve(lambda e, r=r, h=h: e.memset(c.vtok[r][0][:, 65 * h + 64:65 * h + 65], 1.0),
                  writes=[c.vtok[r][1]])

    w16 = [sb(f"wqv{i}", [128, 1024], BF16, st) for i in range(2)]
    vtmp = [sb(f"vtmp{i}", [128, 512], BF16, st) for i in range(2)]

    def proj(wdram, dstQ, biascol, is_v):
        for e_ in range(NC):
            sl = e_ % 2
            kb.gp(lambda e, e_=e_, sl=sl, wdram=wdram: e.dma_start(
                out=w16[sl][0][:], in_=wdram[e_]), writes=[w16[sl][1]])
            for s in range(NSC):
                pb = ps[2 + (s % 2)]
                for k in range(NC):
                    kb.te(lambda e, s=s, k=k, sl=sl, pb=pb: e.matmul(
                        pb[0][:], w16[sl][0][:, 128 * k:128 * (k + 1)],
                        c.XN[k][0][:, 512 * s:512 * (s + 1)],
                        start=(k == 0), stop=(k == NC - 1)),
                        reads=[w16[sl][1], c.XN[k][1]], writes=[pb[1]])
                if not is_v:
                    kb.ve(lambda e, e_=e_, s=s, pb=pb: e.tensor_scalar(
                        out=dstQ[e_][0][:, 512 * s:512 * (s + 1)], in0=pb[0][:],
                        scalar1=biascol[0][:, e_:e_ + 1], scalar2=None, op0=ALU.add),
                        reads=[pb[1], biascol[1]], writes=[dstQ[e_][1]])
                else:
                    vt = vtmp[s % 2]
                    kb.ve(lambda e, e_=e_, s=s, pb=pb, vt=vt: e.tensor_scalar(
                        out=vt[0][:], in0=pb[0][:],
                        scalar1=biascol[0][:, e_:e_ + 1], scalar2=None, op0=ALU.add),
                        reads=[pb[1], biascol[1]], writes=[vt[1]])
                    for b_ in range(4):
                        r = 4 * s + b_
                        half = b_ % 2
                        kb.te(lambda e, vt=vt, b_=b_, half=half: e.transpose(
                            c.pT[0][:, 512 * half:512 * half + 128],
                            vt[0][:, 128 * b_:128 * (b_ + 1)], c.identB[0][:]),
                            reads=[vt[1], c.identB[1]], writes=[c.pT[1]], tr=True)
                        for hh in range(2):
                            h = 2 * e_ + hh
                            kb.ve(lambda e, r=r, h=h, hh=hh, half=half: e.tensor_copy(
                                out=c.vtok[r][0][:, 65 * h:65 * h + 64],
                                in_=c.pT[0][:, 512 * half + 64 * hh:512 * half + 64 * (hh + 1)]),
                                reads=[c.pT[1]], writes=[c.vtok[r][1]])

    proj(c.wqT, c.BIGA, c.bqc, is_v=False)
    proj(c.wvT, None, c.bvc, is_v=True)
    c.tap("q0", c.BIGA[0][0][:], [c.BIGA[0][1]], [128, T], BF16)
    c.tap("vtok0", c.vtok[0][0][:], [c.vtok[0][1]], [128, 1040], BF16)

    bits = sb("bitsH", [6, T], BF16, st)
    for h in range(NH):
        qh, qb = c.BIGA[h // 2]
        r0 = (h % 2) * 64
        for s in range(NSC):
            kb.te(lambda e, h=h, s=s, qh=qh, r0=r0: e.matmul(
                ps[4][0][0:6, :], c.hyp2[0][r0:r0 + 64, :],
                qh[r0:r0 + 64, 512 * s:512 * (s + 1)], start=True, stop=True),
                reads=[qb, c.hyp2[1]], writes=[ps[4][1]])
            kb.ve(lambda e, s=s: e.tensor_scalar(
                out=bits[0][0:6, 512 * s:512 * (s + 1)], in0=ps[4][0][0:6, :],
                scalar1=c.hypb[0][:], scalar2=0.0, op0=ALU.add, op1=ALU.is_ge),
                reads=[ps[4][1], c.hypb[1]], writes=[bits[1]])
            kb.te(lambda e, s=s: e.matmul(
                ps[5][0][0:1, :], c.pow2[0][:], bits[0][:, 512 * s:512 * (s + 1)],
                start=True, stop=True),
                reads=[bits[1], c.pow2[1]], writes=[ps[5][1]])
            kb.ve(lambda e, s=s: e.tensor_copy(
                out=c.rbA[0][0:1, 512 * s:512 * (s + 1)], in_=ps[5][0][0:1, :]),
                reads=[ps[5][1]], writes=[c.rbA[1]])
        for r in range(NTT):
            kb.te(lambda e, r=r: e.transpose(
                ps[6][0][:, 0:1], c.rbA[0][0:1, 128 * r:128 * (r + 1)],
                c.identF[0][0:1, 0:1]),
                reads=[c.rbA[1], c.identF[1]], writes=[ps[6][1]], tr=True)
            kb.ve(lambda e, r=r, h=h: e.tensor_copy(
                out=c.bucketT[0][:, 16 * r + h:16 * r + h + 1], in_=ps[6][0][:, 0:1]),
                reads=[ps[6][1]], writes=[c.bucketT[1]])
        kb.sy(lambda e, h=h: e.dma_start(out=c.drows[h:h + 1, :], in_=c.rbA[0][:]),
              reads=[c.rbA[1]], writes=[c.b_drows])
    c.tap("bucketT", c.bucketT[0][:], [c.bucketT[1]], [128, 256])

    sqbs = [sb(f"qsq{i}", [128, 512], BF16, st) for i in range(2)]
    rowtmp = sb("qnormtmp", [1, T], F32, st)
    for e_ in range(NC):
        for hh in range(2):
            h = 2 * e_ + hh

            def qsq(s, e_=e_):
                q = sqbs[s % 2]
                kb.se(lambda e, e_=e_, s=s, q=q: e.activation(
                    out=q[0][:], in_=c.BIGA[e_][0][:, 512 * s:512 * (s + 1)],
                    func=AF.Square),
                    reads=[c.BIGA[e_][1]], writes=[q[1]])

            def qmm(s, hh=hh):
                q = sqbs[s % 2]
                pq = ps[4 + s % 2]
                kb.te(lambda e, s=s, hh=hh, q=q, pq=pq: e.matmul(
                    pq[0][0:1, :], c.halfones[0][:, hh:hh + 1], q[0][:],
                    start=True, stop=True),
                    reads=[q[1], c.halfones[1]], writes=[pq[1]])
                kb.se(lambda e, s=s, pq=pq: e.activation(
                    out=rowtmp[0][0:1, 512 * s:512 * (s + 1)], in_=pq[0][0:1, :],
                    func=AF.Sqrt),
                    reads=[pq[1]], writes=[rowtmp[1]])

            qsq(0)
            qsq(1)
            for s in range(NSC):
                qmm(s)
                if s + 2 < NSC:
                    qsq(s + 2)
            kb.ve(lambda e, h=h: e.reduce_max(
                c.mrow[0][0:1, h:h + 1], rowtmp[0][:], axis=mybir.AxisListType.X),
                reads=[rowtmp[1]], writes=[c.mrow[1]])
            kb.ve(lambda e, h=h: e.tensor_scalar(
                out=c.msc[0][0:1, h:h + 1], in0=c.mrow[0][0:1, h:h + 1],
                scalar1=MH32, scalar2=None, op0=ALU.mult),
                reads=[c.mrow[1]], writes=[c.msc[1]])
            kb.ve(lambda e, h=h: e.tensor_scalar(
                out=rowtmp[0][:], in0=rowtmp[0][:],
                scalar1=c.msc[0][0:1, h:h + 1], scalar2=1.0, op0=ALU.mult, op1=ALU.add),
                reads=[rowtmp[1], c.msc[1]], writes=[rowtmp[1]])
            kb.sy(lambda e, h=h: e.dma_start(
                out=c.drows[16 + h:17 + h, :], in_=rowtmp[0][:]),
                reads=[rowtmp[1]], writes=[c.b_drows])
    st.close()
    kb.barrier()



def _p3_l0_attn(c, nh=NH):
    kb, sb, ps = c.kb, c.sb, c.ps
    o16 = c.XN
    st = ExitStack()
    browB = [sb(f"brow{i}", [128, T], BF16, st) for i in range(2)]
    mbcB = [sb("mbc0", [128, T], BF16, st)]
    sums64 = sb("sums64", [65, 1024], F32, st)

    cbuf = [sb(f"cbuf{i}", [128, 1024], BF16, st) for i in range(2)]
    abuf = [sb(f"abuf{i}", [128, 1024], F32, st) for i in range(2)]
    wt2 = [sb(f"wt2_{i}", [128, 1024], BF16, st) for i in range(2)]

    def fill_rows(h):
        kb.gp(lambda e, h=h: e.dma_start(
            out=browB[h % 2][0][:],
            in_=c.drows[h:h + 1, :].broadcast_to((128, T))),
            reads=[c.b_drows], writes=[browB[h % 2][1]])

    S1acc = sb("S1acc", [128, 32], F32, st)
    s1bufs = [Buf(f"s1c{i}") for i in range(32)]
    S1c = sb("S1c", [128, 16], F32, st)
    m1cols = sb("m1cols", [128, 16], F32, st)
    m1b = sb("m1b", [128, 16], F32, st)
    m2c = sb("m2c", [128, 16], F32, st)
    m2st = sb("m2st", [16, 128], F32, st)

    fill_rows(0)
    for h in range(nh):
        qh, qb = c.BIGA[h // 2]
        r0 = (h % 2) * 64
        brow = browB[h % 2]
        mbc = mbcB[0]
        if h + 1 < nh:
            fill_rows(h + 1)

        kb.sy(lambda e, h=h: e.dma_start(out=c.rbA[0][:], in_=c.drows[16 + h:17 + h, :]),
              reads=[c.b_drows], writes=[c.rbA[1]])
        for j in range(16):
            kb.te(lambda e, j=j: e.transpose(
                ps[3][0][:, j:j + 1], c.rbA[0][0:1, 128 * j:128 * (j + 1)],
                c.identF[0][0:1, 0:1]),
                reads=[c.rbA[1], c.identF[1]], writes=[ps[3][1]], tr=True)
        kb.ve(lambda e: e.tensor_copy(out=m1cols[0][:], in_=ps[3][0][:, 0:16]),
              reads=[ps[3][1]], writes=[m1cols[1]])
        kb.ve(lambda e: e.tensor_scalar(
            out=m1b[0][:], in0=m1cols[0][:], scalar1=-1.0 / 9.0, scalar2=None,
            op0=ALU.mult), reads=[m1cols[1]], writes=[m1b[1]])

        for g in range(2 * NTT):
            sb_, half = divmod(g, 2)
            bigh = c.bigT[g % 2]
            bb = [ps[2 * (g % 2)][1], ps[2 * (g % 2) + 1][1]]
            for tcl in range(2):
                tc = 2 * half + tcl
                kb.te(lambda e, sb_=sb_, tc=tc, tcl=tcl, bigh=bigh, qh=qh, r0=r0:
                      e.matmul(
                          bigh[:, 512 * tcl:512 * (tcl + 1)],
                          qh[r0:r0 + 64, 128 * sb_:128 * (sb_ + 1)],
                          qh[r0:r0 + 64, 512 * tc:512 * (tc + 1)],
                          start=True, stop=True),
                      reads=[qb], writes=[bb[tcl]])
            s1b = s1bufs[16 * half + sb_]
            kb.se(lambda e, sb_=sb_, half=half, bigh=bigh: e.activation(
                out=bigh[:, 0:1024], in_=bigh[:, 0:1024], func=AF.Exp,
                scale=62.0 / 288.0, bias=m1b[0][:, sb_:sb_ + 1],
                accum_out=S1acc[0][:, 16 * half + sb_:16 * half + sb_ + 1]),
                reads=[m1b[1]], writes=bb + [s1b])
        kb.ve(lambda e: e.tensor_add(S1c[0][:], S1acc[0][:, 0:16], S1acc[0][:, 16:32]),
              reads=s1bufs, writes=[S1c[1]])
        kb.se(lambda e: e.activation(
            out=m2c[0][:], in_=S1c[0][:], func=AF.Ln, scale=9.223372036854776e+18),
            reads=[S1c[1]], writes=[m2c[1]])
        kb.ve(lambda e: e.scalar_tensor_tensor(
            out=m2c[0][:], in0=m2c[0][:], scalar=9.0, in1=m1cols[0][:],
            op0=ALU.mult, op1=ALU.add), reads=[m2c[1], m1cols[1]], writes=[m2c[1]])
        kb.ve(lambda e: e.tensor_scalar(
            out=m2c[0][:], in0=m2c[0][:], scalar1=-393.01445137748897,
            scalar2=None, op0=ALU.add), reads=[m2c[1]], writes=[m2c[1]])
        kb.te(lambda e: e.transpose(
            ps[3][0][0:16, 0:128], m2c[0][:], c.identF[0][:]),
            reads=[m2c[1], c.identF[1]], writes=[ps[3][1]], tr=True)
        kb.ve(lambda e: e.tensor_copy(out=m2st[0][:], in_=ps[3][0][0:16, 0:128]),
              reads=[ps[3][1]], writes=[m2st[1]])
        kb.gp(lambda e: e.dma_start(
            out=c.lnrows[5:6, :].rearrange("p (b x) -> (p b) x", b=16), in_=m2st[0][:]),
            reads=[m2st[1]], writes=[c.b_lnrows])
        kb.sy(lambda e: e.dma_start(out=c.rbA[0][:], in_=c.lnrows[5:6, :]),
              reads=[c.b_lnrows], writes=[c.rbA[1]])
        for s in range(NSC):
            kb.te(lambda e, s=s: e.matmul(
                ps[1][0][:], c.ones1f[0][:], c.rbA[0][0:1, 512 * s:512 * (s + 1)],
                start=True, stop=True), reads=[c.rbA[1], c.ones1f[1]], writes=[ps[1][1]])
            kb.ve(lambda e, s=s, mbc=mbc: e.tensor_copy(
                out=mbc[0][:, 512 * s:512 * (s + 1)], in_=ps[1][0][:]),
                reads=[ps[1][1]], writes=[mbc[1]])

        c.tap(f"mbc_h{h}", mbc[0][:], [mbc[1]], [128, T], BF16)

        def amm(r, S):
            bigh = c.bigT[r % 2]
            for cc in range(2):
                kb.te(lambda e, r=r, S=S, cc=cc, bigh=bigh, qh=qh, r0=r0: e.matmul(
                    bigh[:, 512 * cc:512 * (cc + 1)],
                    qh[r0:r0 + 64, 128 * r:128 * (r + 1)],
                    qh[r0:r0 + 64, 1024 * S + 512 * cc:1024 * S + 512 * (cc + 1)],
                    start=True, stop=True),
                    reads=[qb], writes=[ps[2 * (r % 2) + cc][1]])

        def chain(r, S):
            wt = wt2[r % 2]
            cb = cbuf[r % 2]; ab = abuf[r % 2]
            bigh = c.bigT[r % 2]
            bb = [ps[2 * (r % 2)][1], ps[2 * (r % 2) + 1][1]]
            kb.ve(lambda e, r=r, S=S, cb=cb, h=h, brow=brow: e.tensor_scalar(
                out=cb[0][:], in0=brow[0][:, 1024 * S:1024 * (S + 1)],
                scalar1=c.bucketT[0][:, 16 * r + h:16 * r + h + 1], scalar2=62.0,
                op0=ALU.is_equal, op1=ALU.add),
                reads=[brow[1], c.bucketT[1]], writes=[cb[1]])
            kb.ve(lambda e, cb=cb, bigh=bigh, ab=ab: e.scalar_tensor_tensor(
                out=ab[0][:], in0=cb[0][:], scalar=1.0 / 32.0, in1=bigh[:, 0:1024],
                op0=ALU.mult, op1=ALU.mult),
                reads=[cb[1]] + bb, writes=[ab[1]])
            kb.ve(lambda e, S=S, ab=ab, mbc=mbc: e.tensor_sub(
                ab[0][:], ab[0][:], mbc[0][:, 1024 * S:1024 * (S + 1)]),
                reads=[ab[1], mbc[1]], writes=[ab[1]])
            kb.se(lambda e, ab=ab, wt=wt: e.activation(
                out=wt[0][:], in_=ab[0][:], func=AF.Exp, bias=-0.05),
                reads=[ab[1]], writes=[wt[1]])

        def vmm(r, S):
            wt = wt2[r % 2]
            for cc in range(2):
                kb.te(lambda e, r=r, cc=cc, wt=wt, h=h: e.matmul(
                    ps[4 + cc][0][0:65, :], c.vtok[r][0][:, 65 * h:65 * (h + 1)],
                    wt[0][:, 512 * cc:512 * (cc + 1)],
                    start=(r == 0), stop=(r == NTT - 1)),
                    reads=[wt[1], c.vtok[r][1]], writes=[ps[4 + cc][1]])

        for S in range(2):
            for r in range(NTT):
                amm(r, S)
                if r >= 2:
                    vmm(r - 2, S)
                chain(r, S)
            vmm(NTT - 2, S)
            vmm(NTT - 1, S)
            for cc in range(2):
                col0 = 1024 * S + 512 * cc
                if r0 == 0:
                    kb.se(lambda e, cc=cc, col0=col0, h=h: e.activation(
                        out=o16[h // 2][0][0:64, col0:col0 + 512],
                        in_=ps[4 + cc][0][0:64, :], func=AF.Copy),
                        reads=[ps[4 + cc][1]], writes=[o16[h // 2][1]])
                else:
                    kb.se(lambda e, cc=cc: e.activation(
                        out=wt2[0][0][0:64, 512 * cc:512 * (cc + 1)],
                        in_=ps[4 + cc][0][0:64, :], func=AF.Copy),
                        reads=[ps[4 + cc][1]], writes=[wt2[0][1]])
                kb.ve(lambda e, cc=cc: e.tensor_copy(
                    out=sums64[0][64:65, 512 * cc:512 * (cc + 1)],
                    in_=ps[4 + cc][0][64:65, :]),
                    reads=[ps[4 + cc][1]], writes=[sums64[1]])
            kb.sy(lambda e, h=h, S=S: e.dma_start(
                out=c.drows[32 + h:33 + h, 1024 * S:1024 * (S + 1)],
                in_=sums64[0][64:65, :]),
                reads=[sums64[1]], writes=[c.b_drows])
            if r0 != 0:
                kb.gp(lambda e, h=h, S=S: e.dma_start(
                    out=o16[h // 2][0][64:128, 1024 * S:1024 * (S + 1)],
                    in_=wt2[0][0][0:64, :]),
                    reads=[wt2[0][1]], writes=[o16[h // 2][1]])
    st.close()
    kb.barrier()
    if nh < NH:
        return

    st2 = ExitStack()
    sstack = sb("sstack", [16, T], F32, st2)
    onorm = [sb(f"onorm{i}", [128, 512], F32, st2) for i in range(2)]
    kb.sy(lambda e: e.dma_start(out=sstack[0][:], in_=c.drows[32:48, :]),
          reads=[c.b_drows], writes=[sstack[1]])
    kb.ve(lambda e: e.reciprocal(out=sstack[0][:], in_=sstack[0][:]),
          reads=[sstack[1]], writes=[sstack[1]])

    for h in range(NH):
        r0 = (h % 2) * 64
        kb.gp(lambda e, h=h: e.dma_start(out=c.rbA[0][:], in_=sstack[0][h:h + 1, :]),
              reads=[sstack[1]], writes=[c.rbA[1]])
        for s in range(NSC):
            pr = ps[5 + (s % 2)]
            kb.te(lambda e, s=s, pr=pr: e.matmul(
                pr[0][:], c.ones1f[0][:], c.rbA[0][0:1, 512 * s:512 * (s + 1)],
                start=True, stop=True), reads=[c.rbA[1], c.ones1f[1]], writes=[pr[1]])
            ob = onorm[s % 2]
            kb.ve(lambda e, h=h, s=s, pr=pr, ob=ob, r0=r0: e.tensor_mul(
                ob[0][r0:r0 + 64, :], o16[h // 2][0][r0:r0 + 64, 512 * s:512 * (s + 1)],
                pr[0][r0:r0 + 64, :]),
                reads=[o16[h // 2][1], pr[1]], writes=[ob[1]])
            kb.ve(lambda e, h=h, s=s, ob=ob, r0=r0: e.tensor_add(
                c.X[h // 2][0][r0:r0 + 64, 512 * s:512 * (s + 1)],
                c.X[h // 2][0][r0:r0 + 64, 512 * s:512 * (s + 1)], ob[0][r0:r0 + 64, :]),
                reads=[ob[1], c.X[h // 2][1]], writes=[c.X[h // 2][1]])
    c.tap("x1", c.X[0][0][:], [c.X[0][1]], [128, T])
    st2.close()
    c.vtstack.close()
    kb.barrier()



def _p5_l1_ffn(c):
    kb, sb, ps = c.kb, c.sb, c.ps
    st = ExitStack()
    mfull = sb("mfull", [128, 4096], BF16, st)
    GS = [list(range(12 * g, min(12 * (g + 1), 128))) for g in range((128 + 11) // 12)]
    wg0 = [sb(f"wg0_{o}", [128, 12 * 128], BF16, st) for o in range(8)]
    for o in range(8):
        kb.gp(lambda e, o=o: e.dma_start(
            out=wg0[o][0][:, 0:len(GS[0]) * 128],
            in_=c.wcT[o][:, 128 * GS[0][0]:128 * (GS[0][-1] + 1)]),
            writes=[wg0[o][1]])
    pre = ExitStack()
    srm = sb("srm", [16, 4096], BF16, pre)
    sru = sb("sru", [16, 16384], BF16, pre)
    mstage = sb("mstage", [16, 16384], BF16, pre)
    kb.gp(lambda e: e.dma_start(out=srm[0][:], in_=c.rmTsD), writes=[srm[1]])
    kb.gp(lambda e: e.dma_start(out=sru[0][:], in_=c.ruTsD), writes=[sru[1]])
    kb.ve(lambda e: e.memset(mfull[0][:], 0.0), writes=[mfull[1]])
    for g in range(32):
        pz = ps[g % 2]
        for u in range(8):
            blk = g * 8 + u
            kb.te(lambda e, blk=blk, u=u, pz=pz: e.matmul(
                pz[0][0:16, 64 * u:64 * (u + 1)],
                srm[0][:, 16 * blk:16 * (blk + 1)],
                sru[0][:, 64 * blk:64 * (blk + 1)], start=True, stop=True),
                reads=[srm[1], sru[1]], writes=[pz[1]])
        kb.ve(lambda e, g=g, pz=pz: e.tensor_copy(
            out=mstage[0][0:16, 512 * g:512 * (g + 1)], in_=pz[0][0:16, :]),
            reads=[pz[1]], writes=[mstage[1]])
    for m in range(4):
        for pp_ in range(2):
            s3 = mstage[0][0:16, :].rearrange("p (i c) -> p i c", c=512)
            s3 = s3[:, :, 64 * (2 * m + pp_):64 * (2 * m + pp_) + 64]
            d3 = mfull[0][32 * m + 16 * pp_:32 * m + 16 * (pp_ + 1), :].rearrange(
                "p (i c) -> p i c", c=128)
            d3 = d3[:, :, 64 * pp_:64 * pp_ + 64]
            kb.gp(lambda e, s3=s3, d3=d3: e.dma_start(out=d3, in_=s3),
                  reads=[mstage[1]], writes=[mfull[1]])
    pre.close()
    kb.barrier()

    partial = [sb(f"pf{i}", [128, T], F32, st) for i in range(NC)]
    wg = [wg0, [sb(f"wg1_{o}", [128, 12 * 128], BF16, st) for o in range(8)]]
    hk = [sb(f"hk{k}", [128, 512], BF16, st) for k in range(12)]
    for gi, G in enumerate(GS):
        gb = wg[gi % 2]
        for t4 in range(NSC):
            for ki, j in enumerate(G):
                h_ = j // 32
                jj = j % 32
                m = jj % 4
                ccol = jj // 4 + 8 * h_
                tn = jj // 4
                pz = ps[4 + (ki % 2)]
                kb.te(lambda e, m=m, ccol=ccol, tn=tn, t4=t4, pz=pz: e.matmul(
                    pz[0][:], mfull[0][32 * m:32 * (m + 1), 128 * ccol:128 * (ccol + 1)],
                    c.XN[tn][0][32 * m:32 * (m + 1), 512 * t4:512 * (t4 + 1)],
                    start=True, stop=True, tile_position=(32 * m, 0)),
                    reads=[mfull[1], c.XN[tn][1]], writes=[pz[1]])
                kb.se(lambda e, ki=ki, pz=pz: e.activation(
                    out=hk[ki][0][:], in_=pz[0][:], func=AF.Gelu),
                    reads=[pz[1]], writes=[hk[ki][1]])
            for o in range(8):
                po = ps[2 + (o % 2)]
                for ki in range(len(G)):
                    kb.te(lambda e, o=o, ki=ki, po=po, gb=gb, n=len(G): e.matmul(
                        po[0][:], gb[o][0][:, 128 * ki:128 * (ki + 1)], hk[ki][0][:],
                        start=(ki == 0), stop=(ki == n - 1)),
                        reads=[gb[o][1], hk[ki][1]], writes=[po[1]])
                if gi == 0:
                    kb.ve(lambda e, o=o, t4=t4, po=po: e.tensor_copy(
                        out=partial[o][0][:, 512 * t4:512 * (t4 + 1)], in_=po[0][:]),
                        reads=[po[1]], writes=[partial[o][1]])
                else:
                    kb.ve(lambda e, o=o, t4=t4, po=po: e.tensor_add(
                        partial[o][0][:, 512 * t4:512 * (t4 + 1)],
                        partial[o][0][:, 512 * t4:512 * (t4 + 1)], po[0][:]),
                        reads=[po[1], partial[o][1]], writes=[partial[o][1]])
            if t4 == NSC - 1 and gi + 1 < len(GS):
                nb = wg[(gi + 1) % 2]
                NG2 = GS[gi + 1]
                for o in range(8):
                    kb.gp(lambda e, o=o, nb=nb, NG2=NG2: e.dma_start(
                        out=nb[o][0][:, 0:len(NG2) * 128],
                        in_=c.wcT[o][:, 128 * NG2[0]:128 * (NG2[-1] + 1)]),
                        writes=[nb[o][1]])
    for o in range(8):
        kb.sy(lambda e, o=o: e.dma_start(
            out=c.ffn1sp[128 * o:128 * (o + 1), :], in_=partial[o][0][:]),
            reads=[partial[o][1]], writes=[c.b_ffn1sp])
    c.tap("ffn1", partial[0][0][:], [partial[0][1]], [128, T])
    st.close()
    kb.barrier()



def _p6_l1_attn(c):
    kb, sb, ps = c.kb, c.sb, c.ps
    st = ExitStack()
    Q1 = c.BIGA
    K1 = [sb(f"K1_{i}", [128, T], BF16, st) for i in range(NC)]
    V1 = [sb(f"V1_{r}", [128, 1024], BF16, st) for r in range(NTT)]
    mneg = sb("mneg", [1, T], BF16, st)
    cmask = sb("cmaskS", [128, 2048], BF16, st)
    pstk = ExitStack()
    w16 = [sb(f"w1_{i}", [128, 1024], BF16, pstk) for i in range(2)]
    vtmp = [sb(f"v1tmp{i}", [128, 512], BF16, pstk) for i in range(2)]
    sqb = sb("sq1", [128, 512], BF16, pstk)
    sqc = sb("sq1b", [128, 512], BF16, pstk)
    kb.gp(lambda e: e.dma_start(out=cmask[0][:], in_=c.cmaskD), writes=[cmask[1]])

    def proj1(wdram, dst, to_v):
        for e_ in range(NC):
            sl = e_ % 2
            kb.gp(lambda e, e_=e_, sl=sl, wdram=wdram: e.dma_start(
                out=w16[sl][0][:], in_=wdram[e_]), writes=[w16[sl][1]])
            for s in range(NSC):
                pb = ps[s % 2]
                for k in range(NC):
                    kb.te(lambda e, s=s, k=k, sl=sl, pb=pb: e.matmul(
                        pb[0][:], w16[sl][0][:, 128 * k:128 * (k + 1)],
                        c.XN[k][0][:, 512 * s:512 * (s + 1)],
                        start=(k == 0), stop=(k == NC - 1)),
                        reads=[w16[sl][1], c.XN[k][1]], writes=[pb[1]])
                if not to_v:
                    kb.ve(lambda e, e_=e_, s=s, pb=pb, dst=dst: e.tensor_copy(
                        out=dst[e_][0][:, 512 * s:512 * (s + 1)], in_=pb[0][:]),
                        reads=[pb[1]], writes=[dst[e_][1]])
                else:
                    vt = vtmp[s % 2]
                    kb.ve(lambda e, pb=pb, vt=vt: e.tensor_copy(
                        out=vt[0][:], in_=pb[0][:]), reads=[pb[1]], writes=[vt[1]])
                    for b_ in range(4):
                        r = 4 * s + b_
                        half = b_ % 2
                        kb.te(lambda e, vt=vt, b_=b_, half=half: e.transpose(
                            c.pT[0][:, 512 * half:512 * half + 128],
                            vt[0][:, 128 * b_:128 * (b_ + 1)], c.identB[0][:]),
                            reads=[vt[1], c.identB[1]], writes=[c.pT[1]], tr=True)
                        kb.ve(lambda e, r=r, e_=e_, half=half: e.tensor_copy(
                            out=V1[r][0][:, 128 * e_:128 * (e_ + 1)],
                            in_=c.pT[0][:, 512 * half:512 * half + 128]),
                            reads=[c.pT[1]], writes=[V1[r][1]])

    proj1(c.wk1, K1, to_v=False)
    proj1(c.wq1, Q1, to_v=False)
    proj1(c.wv1, None, to_v=True)

    def normsq(src, rowdst):
        def nsq(s, e_, src=src):
            q = sqb if e_ % 2 == 0 else sqc
            kb.se(lambda e, e_=e_, s=s, src=src, q=q: e.activation(
                out=q[0][:], in_=src[e_][0][:, 512 * s:512 * (s + 1)],
                func=AF.Square), reads=[src[e_][1]], writes=[q[1]])

        for s in range(NSC):
            nsq(s, 0)
            for e_ in range(NC):
                q = sqb if e_ % 2 == 0 else sqc
                if e_ + 1 < NC:
                    nsq(s, e_ + 1)
                kb.te(lambda e, e_=e_, s=s, q=q: e.matmul(
                    ps[2][0][0:1, :], c.ones128[0][:], q[0][:],
                    start=(e_ == 0), stop=(e_ == NC - 1)),
                    reads=[q[1], c.ones128[1]], writes=[ps[2][1]])
            kb.se(lambda e, s=s, rowdst=rowdst: e.activation(
                out=rowdst[0][0:1, 512 * s:512 * (s + 1)], in_=ps[2][0][0:1, :],
                func=AF.Sqrt), reads=[ps[2][1]], writes=[rowdst[1]])

    normsq(K1, c.rbA)
    kb.ve(lambda e: e.reduce_max(c.sca[0][0:1, 0:1], c.rbA[0][:], axis=mybir.AxisListType.X),
          reads=[c.rbA[1]], writes=[c.sca[1]])
    kb.ve(lambda e: e.tensor_scalar(
        out=c.sca[0][0:1, 1:2], in0=c.sca[0][0:1, 0:1], scalar1=-1.02, scalar2=None,
        op0=ALU.mult), reads=[c.sca[1]], writes=[c.sca[1]])
    normsq(Q1, c.rbA)
    kb.ve(lambda e: e.tensor_scalar(
        out=mneg[0][:], in0=c.rbA[0][:], scalar1=c.sca[0][0:1, 1:2], scalar2=-32.0,
        op0=ALU.mult, op1=ALU.add), reads=[c.rbA[1], c.sca[1]], writes=[mneg[1]])
    pstk.close()

    wt1 = [sb(f"wt1_{r}", [128, T - 128 * r], BF16, st) for r in range(NTT)]
    dbuf = [sb(f"dbuf{i}", [128, 512], F32, st) for i in range(2)]
    seqn = 0
    for r in range(NTT):
        for j in range(r // 4, NSC):
            pa = ps[seqn % 4]
            seqn += 1
            for k in range(NC):
                kb.te(lambda e, r=r, j=j, k=k, pa=pa: e.matmul(
                    pa[0][:], K1[k][0][:, 128 * r:128 * (r + 1)],
                    Q1[k][0][:, 512 * j:512 * (j + 1)],
                    start=(k == 0), stop=False),
                    reads=[K1[k][1], Q1[k][1]], writes=[pa[1]])
            kb.te(lambda e, j=j, pa=pa: e.matmul(
                pa[0][:], c.ones1b[0][:], mneg[0][0:1, 512 * j:512 * (j + 1)],
                start=False, stop=True),
                reads=[mneg[1], c.ones1b[1]], writes=[pa[1]])
            if j == r // 4:
                ko = r % 4
                db = dbuf[seqn % 2]
                kb.ve(lambda e, ko=ko, pa=pa, db=db: e.tensor_add(
                    db[0][:], pa[0][:], cmask[0][:, 512 * ko:512 * (ko + 1)]),
                    reads=[pa[1], cmask[1]], writes=[db[1]])
                kb.se(lambda e, r=r, ko=ko, db=db: e.activation(
                    out=wt1[r][0][:, 0:512 - 128 * ko], in_=db[0][:, 128 * ko:512],
                    func=AF.Exp, scale=1.0 / 32.0, bias=-0.03),
                    reads=[db[1]], writes=[wt1[r][1]])
            else:
                col0 = 512 * j - 128 * r
                kb.se(lambda e, r=r, col0=col0, pa=pa: e.activation(
                    out=wt1[r][0][:, col0:col0 + 512], in_=pa[0][:], func=AF.Exp,
                    scale=1.0 / 32.0, bias=-0.03),
                    reads=[pa[1]], writes=[wt1[r][1]])

    for j in range(NSC):
        for r in range(4 * j + 4):
            lo = max(512 * j, 128 * r)
            kb.te(lambda e, r=r, j=j, lo=lo: e.matmul(
                ps[2][0][0:1, lo - 512 * j:512], c.ones128[0][:],
                wt1[r][0][:, lo - 128 * r:512 * (j + 1) - 128 * r],
                start=(r == 0), stop=(r == 4 * j + 3)),
                reads=[wt1[r][1], c.ones128[1]], writes=[ps[2][1]])
        kb.ve(lambda e, j=j: e.tensor_copy(
            out=c.rbA[0][0:1, 512 * j:512 * (j + 1)], in_=ps[2][0][0:1, :]),
            reads=[ps[2][1]], writes=[c.rbA[1]])
    kb.ve(lambda e: e.reciprocal(out=c.rbA[0][:], in_=c.rbA[0][:]),
          reads=[c.rbA[1]], writes=[c.rbA[1]])
    kb.sy(lambda e: e.dma_start(out=c.lnrows[4:5, :], in_=c.rbA[0][:]),
          reads=[c.rbA[1]], writes=[c.b_lnrows])

    rbc = [sb(f"rbc{i}", [128, 512], F32, st) for i in range(2)]
    xbuf = [sb(f"xb{i}", [128, 512], F32, st) for i in range(2)]
    fbuf2 = [sb(f"fb2{i}", [128, 512], F32, st) for i in range(2)]
    ob = [sb(f"obf{i}", [128, 512], F32, st) for i in range(2)]

    def evac(j, d, po):
        rb = rbc[j % 2]
        xb = xbuf[d % 2]; fb = fbuf2[d % 2]; o_ = ob[d % 2]
        kb.ve(lambda e, po=po, o_=o_, rb=rb: e.tensor_mul(o_[0][:], po[0][:], rb[0][:]),
              reads=[po[1], rb[1]], writes=[o_[1]])
        kb.ve(lambda e, o_=o_, xb=xb: e.tensor_add(o_[0][:], o_[0][:], xb[0][:]),
              reads=[o_[1], xb[1]], writes=[o_[1]])
        kb.ve(lambda e, o_=o_, fb=fb: e.tensor_add(o_[0][:], o_[0][:], fb[0][:]),
              reads=[o_[1], fb[1]], writes=[o_[1]])
        kb.sy(lambda e, d=d, j=j, o_=o_: e.dma_start(
            out=c.outT[128 * d:128 * (d + 1), 512 * j:512 * (j + 1)], in_=o_[0][:]),
            reads=[o_[1]], writes=[c.b_outT])

    def fill_rbc(j):
        kb.sy(lambda e, j=j: e.dma_start(
            out=rbc[j % 2][0][:],
            in_=c.lnrows[4:5, 512 * j:512 * (j + 1)].broadcast_to((128, 512))),
            reads=[c.b_lnrows], writes=[rbc[j % 2][1]])

    pend = None
    kb.te(lambda e: e.matmul(
        ps[2][0][:], c.ones1f[0][:], c.rbA[0][0:1, 0:512],
        start=True, stop=True), reads=[c.rbA[1], c.ones1f[1]], writes=[ps[2][1]])
    kb.ve(lambda e: e.tensor_copy(out=rbc[0][0][:], in_=ps[2][0][:]),
          reads=[ps[2][1]], writes=[rbc[0][1]])
    for j in range(NSC):
        for d in range(NC):
            if d == 1 and j + 1 < NSC:
                fill_rbc(j + 1)
            po = ps[3 + (d % 4)]
            xb = xbuf[d % 2]; fb = fbuf2[d % 2]
            kb.sy(lambda e, d=d, j=j, xb=xb: e.dma_start(
                out=xb[0][:], in_=c.xspill[128 * d:128 * (d + 1), 512 * j:512 * (j + 1)]),
                reads=[c.b_xspill], writes=[xb[1]])
            kb.sy(lambda e, d=d, j=j, fb=fb: e.dma_start(
                out=fb[0][:], in_=c.ffn1sp[128 * d:128 * (d + 1), 512 * j:512 * (j + 1)]),
                reads=[c.b_ffn1sp], writes=[fb[1]])
            for r in range(4 * j + 4):
                lo = max(512 * j, 128 * r)
                kb.te(lambda e, r=r, j=j, d=d, po=po, lo=lo: e.matmul(
                    po[0][:, lo - 512 * j:512], V1[r][0][:, 128 * d:128 * (d + 1)],
                    wt1[r][0][:, lo - 128 * r:512 * (j + 1) - 128 * r],
                    start=(r == 0), stop=(r == 4 * j + 3)),
                    reads=[V1[r][1], wt1[r][1]], writes=[po[1]])
            if pend is not None:
                evac(*pend)
            pend = (j, d, po)
    evac(*pend)
    st.close()



def kernel(**inputs):
    maps = host_layout(inputs)
    nc, _c = build_program()
    res = run_bass_kernel_spmd(nc, maps, core_ids=list(range(NCORES)))
    out = np.stack([np.ascontiguousarray(np.asarray(r["outT"]).T) for r in res.results],
                   axis=0)
    return out.astype(np.float32)


# version 17 (final)
# speedup vs baseline: 1.0070x; 1.0070x over previous
import numpy as np
from contextlib import ExitStack

import concourse.bass as bass
import concourse.mybir as mybir
from concourse.bass_utils import run_bass_kernel_spmd

F32 = mybir.dt.float32
BF16 = mybir.dt.bfloat16
FP16 = mybir.dt.float16
FP8 = mybir.dt.float8e4
AF = mybir.ActivationFunctionType
ALU = mybir.AluOpType

NCORES = 8
T = 2048
D = 1024
NTT = 16
NSC = 4
NC = 8
NH = 16
MH32 = (63.0 / 32.0) * 1.02



class Buf:
    __slots__ = ("name", "w", "r")

    def __init__(self, name):
        self.name = name
        self.w = None
        self.r = []


_NQ = 8


class Stream:
    def __init__(self, name, is_dma):
        self.name = name
        self.is_dma = is_dma
        self.items = []


class KB:
    def __init__(self, nc):
        self.nc = nc
        self.st = {n: Stream(n, n in ("gp", "sy"))
                   for n in ("te", "ve", "se", "gp", "sy")}
        self._bar = None
        self._bar_done = set()

    def op(self, sname, fn, reads=(), writes=(), tr=False):
        st = self.st[sname]
        seq = len(st.items)
        deps = set()
        if self._bar is not None and sname not in self._bar_done:
            for bn, bseq in self._bar.items():
                if bseq >= 0:
                    deps.add((self.st[bn], bseq))
            self._bar_done.add(sname)
            if len(self._bar_done) == len(self.st):
                self._bar = None
        for b in reads:
            if b.w is not None:
                deps.add(b.w)
        for b in writes:
            if b.w is not None:
                deps.add(b.w)
            deps.update(b.r)
        deps = {d for d in deps if not (d[0] is st and d[1] >= seq)}
        if sname == "te":
            deps = {d for d in deps if d[0] is not st}
        dmax = {}
        for (pst, pseq) in deps:
            key = (pst, pseq % 8) if pst.is_dma else pst
            if dmax.get(key, (None, -1))[1] < pseq:
                dmax[key] = (pst, pseq)
        deps = set(dmax.values())
        st.items.append([fn, deps, st.is_dma, tr])
        pos = (st, seq)
        for b in reads:
            b.r.append(pos)
        for b in writes:
            b.w = pos
            b.r = []
        return pos

    def te(self, fn, reads=(), writes=(), tr=False):
        return self.op("te", fn, reads, writes, tr)

    def ve(self, fn, reads=(), writes=()):
        return self.op("ve", fn, reads, writes)

    def se(self, fn, reads=(), writes=()):
        return self.op("se", fn, reads, writes)

    def gp(self, fn, reads=(), writes=()):
        return self.op("gp", fn, reads, writes)

    def sy(self, fn, reads=(), writes=()):
        return self.op("sy", fn, reads, writes)

    def barrier(self):
        self._bar = {n: len(s.items) - 1 for n, s in self.st.items()}
        self._bar_done = set()

    def finalize(self):
        nc = self.nc

        waitplans = {n: [] for n in self.st}
        know = {}
        K = {n: {} for n in self.st}
        ptr = {n: 0 for n in self.st}
        done = {n: -1 for n in self.st}
        prog = True
        while prog:
            prog = False
            for n, st in self.st.items():
                while ptr[n] < len(st.items):
                    it = st.items[ptr[n]]
                    if not all(done[pst.name] >= pseq for (pst, pseq) in it[1]):
                        break
                    k = ptr[n]
                    Ks = K[n]
                    if st.is_dma and k >= _NQ:
                        key = (n, (k - _NQ) % _NQ)
                        if Ks.get(key, -1) < k - _NQ:
                            Ks[key] = k - _NQ
                        kn = know.get((n, k - _NQ))
                        if kn:
                            for kk, vv in kn.items():
                                if Ks.get(kk, -1) < vv:
                                    Ks[kk] = vv
                    deps = sorted(it[1], key=lambda d: (d[0] is st, d[0].name, d[1]))
                    if (n == "te" and k > 0 and
                            (it[3] or st.items[k - 1][3])):
                        deps = deps + [(st, k - 1)]
                    waits = []
                    for (pst, pseq) in deps:
                        key = (pst.name, pseq % _NQ) if pst.is_dma else pst.name
                        if Ks.get(key, -1) >= pseq:
                            continue
                        waits.append((pst, pseq))
                        Ks[key] = pseq
                        kn = know.get((pst.name, pseq))
                        if kn:
                            for kk, vv in kn.items():
                                if Ks.get(kk, -1) < vv:
                                    Ks[kk] = vv
                    snap = dict(Ks)
                    skey = (n, k % _NQ) if st.is_dma else n
                    if snap.get(skey, -1) < k:
                        snap[skey] = k
                    know[(n, k)] = snap
                    waitplans[n].append(waits)
                    done[n] = k
                    ptr[n] += 1
                    prog = True
        if any(ptr[n] < len(self.st[n].items) for n in self.st):
            raise RuntimeError("waitplan scheduling stuck")
        self.waitplans = waitplans

        for st in self.st.values():
            for it in st.items:
                it[2] = st.is_dma
        for n in self.st:
            for waits in waitplans[n]:
                for (pst, pseq) in waits:
                    pst.items[pseq][2] = True
        for st in self.st.values():
            cnt = 0
            st.inc_at = []
            for it in st.items:
                if it[2]:
                    cnt += 16 if st.is_dma else 1
                st.inc_at.append(cnt)
            st.total = cnt
        ptr = {n: 0 for n in self.st}
        done = {n: -1 for n in self.st}
        prog = True
        while prog:
            prog = False
            for n, st in self.st.items():
                while ptr[n] < len(st.items):
                    if all(done[pst.name] >= pseq for (pst, pseq) in st.items[ptr[n]][1]):
                        done[n] = ptr[n]
                        ptr[n] += 1
                        prog = True
                    else:
                        break
        stuck = [n for n in self.st if ptr[n] < len(self.st[n].items)]
        if stuck:
            msg = []
            for n in stuck:
                it = self.st[n].items[ptr[n]]
                unmet = [(pst.name, pseq) for (pst, pseq) in it[1] if done[pst.name] < pseq]
                msg.append(f"stream {n} stuck at {ptr[n]} waiting {unmet}")
            raise RuntimeError("DEADLOCK:\n" + "\n".join(msg))

        NQ = 8
        es = ExitStack()
        sems = {}
        for n, st in self.st.items():
            if st.is_dma:
                sems[n] = [es.enter_context(nc.semaphore(f"sem_{n}{q}")) for q in range(NQ)]
            else:
                sems[n] = es.enter_context(nc.semaphore(f"sem_{n}"))
        kbself = self

        def dep_target(pst, pseq):
            if pst.is_dma:
                return (sems[pst.name][pseq % NQ], 16 * (pseq // NQ + 1))
            return (sems[pst.name], pst.inc_at[pseq])

        def emit_stream(st, eng):
            plans = kbself.waitplans[st.name]
            for k, it in enumerate(st.items):
                if st.is_dma and k >= NQ:
                    sem, val = dep_target(st, k - NQ)
                    eng.wait_ge(sem, val)
                for (pst, pseq) in plans[k]:
                    sem, val = dep_target(pst, pseq)
                    if val > 0:
                        eng.wait_ge(sem, val)
                inst = it[0](eng)
                if it[2]:
                    if st.is_dma:
                        inst.then_inc(sems[st.name][k % NQ], 16)
                    else:
                        inst.then_inc(sems[st.name], 1)
            if st.is_dma:
                n = len(st.items)
                for q in range(NQ):
                    cnt = 16 * ((n - q + NQ - 1) // NQ)
                    if cnt > 0:
                        eng.wait_ge(sems[st.name][q], cnt)

        with es, nc.Block() as block:
            @block.tensor
            def _(eng):
                emit_stream(kbself.st["te"], eng)

            @block.vector
            def _(eng):
                emit_stream(kbself.st["ve"], eng)

            @block.scalar
            def _(eng):
                emit_stream(kbself.st["se"], eng)

            @block.gpsimd
            def _(eng):
                emit_stream(kbself.st["gp"], eng)

            @block.sync
            def _(eng):
                emit_stream(kbself.st["sy"], eng)



def colblocks(A):
    K, M = A.shape
    return np.ascontiguousarray(
        A.reshape(K // 128, 128, M // 128, 128).transpose(2, 1, 0, 3).reshape(M // 128, 128, K))


def make_causal_masks():
    p = np.arange(128)[:, None]
    j = np.arange(512)[None, :]
    m = np.zeros((128, 4, 512), dtype=np.float32)
    for k in range(4):
        m[:, k, :] = np.where(j < 128 * k + p, np.float32(-1e9), np.float32(0.0))
    return np.ascontiguousarray(m.reshape(128, 2048))


def host_layout(inputs):
    x = np.asarray(inputs["L0_attn_x"], dtype=np.float32)
    hyp = np.asarray(inputs["L0_attn_hyperplanes"], dtype=np.float32)
    rm = np.asarray(inputs["L1_ffn_rail_maker"], dtype=np.float32)
    ru = np.asarray(inputs["L1_ffn_rail_up_proj"], dtype=np.float32)
    shared = {
        "wqT": colblocks(np.ascontiguousarray(np.asarray(inputs["L0_attn_Wq"], np.float32).T)),
        "wvT": colblocks(np.ascontiguousarray(np.asarray(inputs["L0_attn_Wv"], np.float32).T)),
        "wfT": colblocks(np.ascontiguousarray(np.asarray(inputs["L0_ffn_weight"], np.float32).T)),
        "wk1": colblocks(np.asarray(inputs["L1_attn_Wk"], np.float32)),
        "wq1": colblocks(np.asarray(inputs["L1_attn_Wq"], np.float32)),
        "wv1": colblocks(np.asarray(inputs["L1_attn_Wv"], np.float32)),
        "wcT": colblocks(np.ascontiguousarray(np.asarray(inputs["L1_ffn_w_cproj"], np.float32).T)),
        "hyp2": np.ascontiguousarray(np.concatenate([hyp[:64], hyp[:64]], axis=0)),
        "hypb": np.ascontiguousarray(hyp[64:65].T),
        "pow2": np.ascontiguousarray((2.0 ** np.arange(6)).astype(np.float32)[:, None]),
        "bqc": np.ascontiguousarray(np.asarray(inputs["L0_attn_bq"], np.float32).reshape(8, 128).T),
        "bvc": np.ascontiguousarray(np.asarray(inputs["L0_attn_bv"], np.float32).reshape(8, 128).T),
        "bfc": np.ascontiguousarray(np.asarray(inputs["L0_ffn_bias"], np.float32).reshape(8, 128).T),
        "identD": np.eye(128, dtype=np.float32),
        "cmaskD": make_causal_masks(),
        "rmTs": np.ascontiguousarray(rm.transpose(3, 0, 1, 2).reshape(16, 4096)),
        "ruTs": np.ascontiguousarray(ru.transpose(3, 0, 1, 2).reshape(16, 16384)),
    }
    maps = []
    for b in range(NCORES):
        m = dict(shared)
        m["xT"] = np.ascontiguousarray(x[b].T)
        maps.append(m)
    return maps



class Ctx:
    pass


def build_program(cfg=None):
    cfg = dict(cfg or {})
    phases = cfg.get("phases", 6)
    taps = set(cfg.get("taps", ()))

    nc = bass.Bass("TRN2", target_bir_lowering=False)
    kb = KB(nc)
    c = Ctx()
    c.nc, c.kb, c.taps = nc, kb, taps
    c.tap_names = []

    def din(name, shape):
        return nc.dram_tensor(name, shape, F32, kind="ExternalInput").ap()

    c.xT = din("xT", [D, T])
    c.wqT = din("wqT", [8, 128, 1024]); c.wvT = din("wvT", [8, 128, 1024])
    c.wfT = din("wfT", [8, 128, 1024])
    c.wk1 = din("wk1", [8, 128, 1024]); c.wq1 = din("wq1", [8, 128, 1024])
    c.wv1 = din("wv1", [8, 128, 1024])
    c.wcT = din("wcT", [8, 128, 16384])
    c.hyp2D = din("hyp2", [128, 6]); c.hypbD = din("hypb", [6, 1]); c.pow2D = din("pow2", [6, 1])
    c.bqcD = din("bqc", [128, 8]); c.bvcD = din("bvc", [128, 8]); c.bfcD = din("bfc", [128, 8])
    c.identD = din("identD", [128, 128]); c.cmaskD = din("cmaskD", [128, 2048])
    c.rmTsD = din("rmTs", [16, 4096]); c.ruTsD = din("ruTs", [16, 16384])

    c.outT = nc.dram_tensor("outT", [D, T], F32, kind="ExternalOutput").ap()
    c.b_outT = Buf("outT")
    c.xspill = nc.dram_tensor("xspill", [D, T], F32, kind="Internal").ap()
    c.b_xspill = Buf("xspill")
    c.ffn1sp = nc.dram_tensor("ffn1sp", [D, T], F32, kind="Internal").ap()
    c.b_ffn1sp = Buf("ffn1sp")
    c.fpart = nc.dram_tensor("fpart", [D, T], F32, kind="Internal").ap()
    c.b_fpart = Buf("fpart")
    c.drows = nc.dram_tensor("drows", [64, T], F32, kind="Internal").ap()
    c.b_drows = Buf("drows")
    c.lnrows = nc.dram_tensor("lnrows", [8, T], F32, kind="Internal").ap()
    c.b_lnrows = Buf("lnrows")
    c.rows16 = nc.dram_tensor("rows16", [4, T], BF16, kind="Internal").ap()
    c.b_rows16 = Buf("rows16")

    def tap(name, ap_src, reads, shape, dtype=F32):
        if name not in taps:
            return
        t = nc.dram_tensor("tap_" + name, shape, dtype, kind="ExternalOutput").ap()
        c.tap_names.append("tap_" + name)
        kb.sy(lambda e, t=t, a=ap_src: e.dma_start(out=t, in_=a), reads=list(reads))
    c.tap = tap

    top = ExitStack()
    c.top = top

    def sb(name, shape, dtype, stack=top):
        h = stack.enter_context(nc.sbuf_tensor(name, shape, dtype))
        return h, Buf(name)

    def pp(name, shape, dtype, stack=top):
        h = stack.enter_context(nc.psum_tensor(name, shape, dtype))
        return h, Buf(name)

    c.sb, c.pp = sb, pp

    c.XN = [sb(f"XN{i}", [128, T], BF16) for i in range(NC)]
    c.BIGA = [sb(f"BA{i}", [128, T], BF16) for i in range(NC)]
    c.rbA = sb("rbA", [1, T], F32)
    c.mrow = sb("mrow", [1, 16], F32)
    c.msc = sb("msc", [1, 16], F32)
    c.sca = sb("sca", [1, 4], F32)
    c.ones1b = sb("ones1b", [1, 128], BF16)
    c.ones1f = sb("ones1f", [1, 128], F32)
    c.ones128 = sb("ones128", [128, 1], BF16)
    c.onesFsq = sb("onesFsq", [128, 128], F32)
    c.halfones = sb("halfones", [128, 2], BF16)
    c.identB = sb("identB", [128, 128], BF16)
    c.identF = sb("identF", [128, 128], F32)
    c.hyp2 = sb("hyp2S", [128, 6], BF16)
    c.hypb = sb("hypbS", [6, 1], F32)
    c.pow2 = sb("pow2S", [6, 1], BF16)
    c.bqc = sb("bqcS", [128, 8], F32)
    c.bvc = sb("bvcS", [128, 8], F32)
    c.bfc = sb("bfcS", [128, 8], F32)
    c.bucketT = sb("bucketT", [128, 256], F32)
    c.fcst = sb("fcst", [128, 10], F32)
    _CONSTVALS = [0.0, 1.0, 1e-5, 1.0 / 32.0, -0.05, -0.03, 0.1111111111111111,
                  9.223372036854776e+18,
                  62.0 / 288.0]
    for k, v in enumerate(_CONSTVALS):
        nc.const_aps.register(F32, v, c.fcst[0][:, k:k + 1])
    c.constvals = _CONSTVALS

    c.bigT = [top.enter_context(nc.psum_tensor(f"BG{i}", [128, 1024], F32))
              for i in range(2)]
    c.ps = []
    for i in range(4):
        hb = c.bigT[i // 2]
        c.ps.append((hb[:, 512 * (i % 2):512 * (i % 2) + 512], Buf(f"PS{i}")))
    c.ps += [pp(f"PS{i}", [128, 512], F32) for i in range(4, 7)]
    c.pT = pp("PT", [128, 1024], BF16)

    c.xstack = ExitStack()
    c.X = [sb(f"X{i}", [128, T], F32, c.xstack) for i in range(NC)]

    _init_consts(c)
    _load_x(c)

    if phases >= 1:
        _p1_ln(c)
    if phases >= 2:
        _p2_ffn_qv(c)
    if phases >= 3:
        _p3_l0_attn(c, nh=cfg.get("p3_nh", NH))
    if phases >= 4:
        _p4_ln2_spill(c)
    if phases >= 5:
        _p5_l1_ffn(c)
    if phases >= 6:
        _p6_l1_attn(c)

    kb.finalize()
    for sname in ("vtstack", "xnqstack", "xstack"):
        s = getattr(c, sname, None)
        if s is not None:
            s.close()
    top.close()
    return nc, c


def _init_consts(c):
    kb = c.kb
    kb.ve(lambda e: e.memset(c.ones1b[0][:], 1.0), writes=[c.ones1b[1]])
    kb.ve(lambda e: e.memset(c.ones1f[0][:], 1.0), writes=[c.ones1f[1]])
    kb.ve(lambda e: e.memset(c.ones128[0][:], 1.0), writes=[c.ones128[1]])
    kb.ve(lambda e: e.memset(c.onesFsq[0][:], 1.0), writes=[c.onesFsq[1]])
    kb.ve(lambda e: e.memset(c.halfones[0][:], 0.0), writes=[c.halfones[1]])
    kb.ve(lambda e: e.memset(c.halfones[0][0:64, 0:1], 1.0), writes=[c.halfones[1]])
    kb.ve(lambda e: e.memset(c.halfones[0][64:128, 1:2], 1.0), writes=[c.halfones[1]])
    kb.gp(lambda e: e.dma_start(out=c.identB[0][:], in_=c.identD), writes=[c.identB[1]])
    kb.sy(lambda e: e.dma_start(out=c.identF[0][:], in_=c.identD), writes=[c.identF[1]])
    kb.gp(lambda e: e.dma_start(out=c.hyp2[0][:], in_=c.hyp2D), writes=[c.hyp2[1]])
    kb.sy(lambda e: e.dma_start(out=c.hypb[0][:], in_=c.hypbD), writes=[c.hypb[1]])
    kb.gp(lambda e: e.dma_start(out=c.pow2[0][:], in_=c.pow2D), writes=[c.pow2[1]])
    kb.sy(lambda e: e.dma_start(out=c.bqc[0][:], in_=c.bqcD), writes=[c.bqc[1]])
    kb.sy(lambda e: e.dma_start(out=c.bvc[0][:], in_=c.bvcD), writes=[c.bvc[1]])
    kb.sy(lambda e: e.dma_start(out=c.bfc[0][:], in_=c.bfcD), writes=[c.bfc[1]])
    for k, v in enumerate(c.constvals):
        kb.ve(lambda e, k=k, v=v: e.memset(c.fcst[0][:, k:k + 1], v), writes=[c.fcst[1]])
    kb.barrier()


def _load_x(c):
    kb = c.kb
    for i in range(NC):
        h, b = c.X[i]
        kb.sy(lambda e, h=h, i=i: e.dma_start(out=h[:], in_=c.xT[128 * i:128 * (i + 1), :]),
              writes=[b])



_LN_N = [0]


def _ln(c, dst, fp8_dst, stack):
    kb, sb, ps = c.kb, c.sb, c.ps
    st = stack
    _LN_N[0] += 1
    tag = f"ln{_LN_N[0]}"
    sqb = sb(tag + "_sq", [128, 512], BF16, st)
    mean = sb(tag + "_mean", [1, T], F32, st)
    msq = sb(tag + "_msq", [1, T], F32, st)
    m2 = sb(tag + "_m2", [1, T], F32, st)
    tb = [sb(f"{tag}_tb{i}", [128, 512], F32, st) for i in range(2)]

    for i in range(NC):
        kb.ve(lambda e, i=i: e.tensor_copy(out=dst[i][0][:], in_=c.X[i][0][:]),
              reads=[c.X[i][1]], writes=[dst[i][1]])
    for s in range(NSC):
        for i in range(NC):
            kb.te(lambda e, i=i, s=s: e.matmul(
                ps[0][0][0:1, :], c.ones128[0][:], dst[i][0][:, 512 * s:512 * (s + 1)],
                start=(i == 0), stop=(i == NC - 1)),
                reads=[dst[i][1], c.ones128[1]], writes=[ps[0][1]])
        kb.ve(lambda e, s=s: e.tensor_scalar(
            out=mean[0][0:1, 512 * s:512 * (s + 1)], in0=ps[0][0][0:1, :],
            scalar1=1.0 / 1024.0, scalar2=None, op0=ALU.mult),
            reads=[ps[0][1]], writes=[mean[1]])
    for s in range(NSC):
        for i in range(NC):
            kb.se(lambda e, i=i, s=s: e.activation(
                out=sqb[0][:], in_=c.X[i][0][:, 512 * s:512 * (s + 1)], func=AF.Square),
                reads=[c.X[i][1]], writes=[sqb[1]])
            kb.te(lambda e, i=i, s=s: e.matmul(
                ps[1][0][0:1, :], c.ones128[0][:], sqb[0][:],
                start=(i == 0), stop=(i == NC - 1)),
                reads=[sqb[1], c.ones128[1]], writes=[ps[1][1]])
        kb.ve(lambda e, s=s: e.tensor_scalar(
            out=msq[0][0:1, 512 * s:512 * (s + 1)], in0=ps[1][0][0:1, :],
            scalar1=1.0 / 1024.0, scalar2=None, op0=ALU.mult),
            reads=[ps[1][1]], writes=[msq[1]])
    kb.ve(lambda e: e.tensor_mul(m2[0][:], mean[0][:], mean[0][:]),
          reads=[mean[1]], writes=[m2[1]])
    kb.ve(lambda e: e.tensor_sub(m2[0][:], msq[0][:], m2[0][:]),
          reads=[msq[1], m2[1]], writes=[m2[1]])
    kb.se(lambda e: e.activation(out=m2[0][:], in_=m2[0][:], func=AF.Sqrt, bias=1e-5),
          reads=[m2[1]], writes=[m2[1]])
    kb.ve(lambda e: e.reciprocal(out=msq[0][:], in_=m2[0][:]),
          reads=[m2[1]], writes=[msq[1]])
    kb.ve(lambda e: e.tensor_mul(m2[0][:], mean[0][:], msq[0][:]),
          reads=[mean[1], msq[1]], writes=[m2[1]])
    for s in range(NSC):
        pr = ps[2 * (s % 2)]
        pm = ps[2 * (s % 2) + 1]
        kb.te(lambda e, s=s, pr=pr: e.matmul(
            pr[0][:], c.ones1f[0][:], msq[0][0:1, 512 * s:512 * (s + 1)],
            start=True, stop=True), reads=[msq[1], c.ones1f[1]], writes=[pr[1]])
        kb.te(lambda e, s=s, pm=pm: e.matmul(
            pm[0][:], c.ones1f[0][:], m2[0][0:1, 512 * s:512 * (s + 1)],
            start=True, stop=True), reads=[m2[1], c.ones1f[1]], writes=[pm[1]])
        for i in range(NC):
            tbi = tb[i % 2]
            kb.ve(lambda e, i=i, s=s, tbi=tbi, pr=pr: e.tensor_mul(
                tbi[0][:], c.X[i][0][:, 512 * s:512 * (s + 1)], pr[0][:]),
                reads=[c.X[i][1], pr[1]], writes=[tbi[1]])
            kb.ve(lambda e, i=i, s=s, tbi=tbi, pm=pm: e.tensor_sub(
                dst[i][0][:, 512 * s:512 * (s + 1)], tbi[0][:], pm[0][:]),
                reads=[tbi[1], pm[1]], writes=[dst[i][1]])
            if fp8_dst is not None:
                kb.ve(lambda e, i=i, s=s, tbi=tbi, pm=pm: e.scalar_tensor_tensor(
                    out=fp8_dst[i][0][:, 512 * s:512 * (s + 1)], in0=tbi[0][:],
                    scalar=0.0, in1=pm[0][:], op0=ALU.add, op1=ALU.subtract),
                    reads=[tbi[1], pm[1]], writes=[fp8_dst[i][1]])


def _p1_ln(c):
    c.xnqstack = ExitStack()
    c.XNQ = [c.sb(f"XQ{i}", [128, T], FP8, c.xnqstack) for i in range(NC)]
    st = ExitStack()
    _ln(c, c.XN, c.XNQ, st)
    c.tap("xn0", c.XN[0][0][:], [c.XN[0][1]], [128, T], BF16)
    c.tap("xq0", c.XNQ[0][0][:], [c.XNQ[0][1]], [128, T], FP8)
    st.close()
    c.kb.barrier()


def _p4_ln2_spill(c):
    kb = c.kb
    st = ExitStack()
    _ln(c, c.XN, None, st)
    c.tap("xn1", c.XN[0][0][:], [c.XN[0][1]], [128, T], BF16)
    for i in range(NC):
        kb.sy(lambda e, i=i: e.dma_start(
            out=c.xspill[128 * i:128 * (i + 1), :], in_=c.X[i][0][:]),
            reads=[c.X[i][1]], writes=[c.b_xspill])
    st.close()
    c.xstack.close()
    kb.barrier()



def _p2_ffn_qv(c):
    kb, sb, ps = c.kb, c.sb, c.ps
    st = ExitStack()

    xq16 = c.BIGA
    for i in range(NC):
        kb.ve(lambda e, i=i: e.tensor_copy(out=xq16[i][0][:], in_=c.XNQ[i][0][:]),
              reads=[c.XNQ[i][1]], writes=[xq16[i][1]])
    ffst = ExitStack()
    wf32 = [sb(f"wf32_{i}", [128, 1024], F32, ffst) for i in range(2)]
    wf8 = [sb(f"wf8_{i}", [128, 1024], FP8, ffst) for i in range(2)]
    wf16 = [sb(f"wf16_{i}", [128, 1024], BF16, ffst) for i in range(2)]
    fbuf = [sb(f"fbuf{i}", [128, 512], F32, ffst) for i in range(2)]
    for e_ in range(NC):
        sl = e_ % 2
        kb.sy(lambda e, e_=e_, sl=sl: e.dma_start(out=wf32[sl][0][:], in_=c.wfT[e_]),
              writes=[wf32[sl][1]])
        kb.ve(lambda e, sl=sl: e.tensor_scalar(
            out=wf8[sl][0][:], in0=wf32[sl][0][:], scalar1=448.0, scalar2=-448.0,
            op0=ALU.min, op1=ALU.max), reads=[wf32[sl][1]], writes=[wf8[sl][1]])
        kb.ve(lambda e, sl=sl: e.tensor_copy(out=wf16[sl][0][:], in_=wf8[sl][0][:]),
              reads=[wf8[sl][1]], writes=[wf16[sl][1]])
        for s in range(NSC):
            pb = ps[s]
            for k in range(NC):
                kb.te(lambda e, s=s, k=k, sl=sl, pb=pb: e.matmul(
                    pb[0][:], wf16[sl][0][:, 128 * k:128 * (k + 1)],
                    xq16[k][0][:, 512 * s:512 * (s + 1)],
                    start=(k == 0), stop=(k == NC - 1)),
                    reads=[wf16[sl][1], xq16[k][1]], writes=[pb[1]])
            fb = fbuf[s % 2]
            kb.ve(lambda e, e_=e_, pb=pb, fb=fb: e.tensor_scalar(
                out=fb[0][:], in0=pb[0][:], scalar1=c.bfc[0][:, e_:e_ + 1],
                scalar2=None, op0=ALU.add),
                reads=[pb[1], c.bfc[1]], writes=[fb[1]])
            kb.ve(lambda e, e_=e_, s=s, fb=fb: e.tensor_add(
                c.X[e_][0][:, 512 * s:512 * (s + 1)],
                c.X[e_][0][:, 512 * s:512 * (s + 1)], fb[0][:]),
                reads=[fb[1], c.X[e_][1]], writes=[c.X[e_][1]])
    ffst.close()
    c.xnqstack.close()
    kb.barrier()
    c.tap("xffn0", c.X[0][0][:], [c.X[0][1]], [128, T])

    c.vtstack = ExitStack()
    c.vtok = [sb(f"VT{r}", [128, 1040], BF16, c.vtstack) for r in range(NTT)]
    for r in range(NTT):
        for h in range(NH):
            kb.ve(lambda e, r=r, h=h: e.memset(c.vtok[r][0][:, 65 * h + 64:65 * h + 65], 1.0),
                  writes=[c.vtok[r][1]])

    w16 = [sb(f"wqv{i}", [128, 1024], BF16, st) for i in range(2)]
    vtmp = [sb(f"vtmp{i}", [128, 512], BF16, st) for i in range(2)]

    def proj(wdram, dstQ, biascol, is_v):
        for e_ in range(NC):
            sl = e_ % 2
            kb.gp(lambda e, e_=e_, sl=sl, wdram=wdram: e.dma_start(
                out=w16[sl][0][:], in_=wdram[e_]), writes=[w16[sl][1]])
            for s in range(NSC):
                pb = ps[s]
                for k in range(NC):
                    kb.te(lambda e, s=s, k=k, sl=sl, pb=pb: e.matmul(
                        pb[0][:], w16[sl][0][:, 128 * k:128 * (k + 1)],
                        c.XN[k][0][:, 512 * s:512 * (s + 1)],
                        start=(k == 0), stop=(k == NC - 1)),
                        reads=[w16[sl][1], c.XN[k][1]], writes=[pb[1]])
                if not is_v:
                    kb.ve(lambda e, e_=e_, s=s, pb=pb: e.tensor_scalar(
                        out=dstQ[e_][0][:, 512 * s:512 * (s + 1)], in0=pb[0][:],
                        scalar1=biascol[0][:, e_:e_ + 1], scalar2=None, op0=ALU.add),
                        reads=[pb[1], biascol[1]], writes=[dstQ[e_][1]])
                else:
                    vt = vtmp[s % 2]
                    kb.ve(lambda e, e_=e_, s=s, pb=pb, vt=vt: e.tensor_scalar(
                        out=vt[0][:], in0=pb[0][:],
                        scalar1=biascol[0][:, e_:e_ + 1], scalar2=None, op0=ALU.add),
                        reads=[pb[1], biascol[1]], writes=[vt[1]])
                    for b_ in range(4):
                        r = 4 * s + b_
                        half = b_ % 2
                        kb.te(lambda e, vt=vt, b_=b_, half=half: e.transpose(
                            c.pT[0][:, 512 * half:512 * half + 128],
                            vt[0][:, 128 * b_:128 * (b_ + 1)], c.identB[0][:]),
                            reads=[vt[1], c.identB[1]], writes=[c.pT[1]], tr=True)
                        for hh in range(2):
                            h = 2 * e_ + hh
                            kb.ve(lambda e, r=r, h=h, hh=hh, half=half: e.tensor_copy(
                                out=c.vtok[r][0][:, 65 * h:65 * h + 64],
                                in_=c.pT[0][:, 512 * half + 64 * hh:512 * half + 64 * (hh + 1)]),
                                reads=[c.pT[1]], writes=[c.vtok[r][1]])

    proj(c.wqT, c.BIGA, c.bqc, is_v=False)
    proj(c.wvT, None, c.bvc, is_v=True)
    c.tap("q0", c.BIGA[0][0][:], [c.BIGA[0][1]], [128, T], BF16)
    c.tap("vtok0", c.vtok[0][0][:], [c.vtok[0][1]], [128, 1040], BF16)

    bits = sb("bitsH", [6, T], BF16, st)
    for h in range(NH):
        qh, qb = c.BIGA[h // 2]
        r0 = (h % 2) * 64
        for s in range(NSC):
            kb.te(lambda e, h=h, s=s, qh=qh, r0=r0: e.matmul(
                ps[4][0][0:6, :], c.hyp2[0][r0:r0 + 64, :],
                qh[r0:r0 + 64, 512 * s:512 * (s + 1)], start=True, stop=True),
                reads=[qb, c.hyp2[1]], writes=[ps[4][1]])
            kb.ve(lambda e, s=s: e.tensor_scalar(
                out=bits[0][0:6, 512 * s:512 * (s + 1)], in0=ps[4][0][0:6, :],
                scalar1=c.hypb[0][:], scalar2=0.0, op0=ALU.add, op1=ALU.is_ge),
                reads=[ps[4][1], c.hypb[1]], writes=[bits[1]])
            kb.te(lambda e, s=s: e.matmul(
                ps[5][0][0:1, :], c.pow2[0][:], bits[0][:, 512 * s:512 * (s + 1)],
                start=True, stop=True),
                reads=[bits[1], c.pow2[1]], writes=[ps[5][1]])
            kb.ve(lambda e, s=s: e.tensor_copy(
                out=c.rbA[0][0:1, 512 * s:512 * (s + 1)], in_=ps[5][0][0:1, :]),
                reads=[ps[5][1]], writes=[c.rbA[1]])
        for r in range(NTT):
            kb.te(lambda e, r=r: e.transpose(
                ps[6][0][:, 0:1], c.rbA[0][0:1, 128 * r:128 * (r + 1)],
                c.identF[0][0:1, 0:1]),
                reads=[c.rbA[1], c.identF[1]], writes=[ps[6][1]], tr=True)
            kb.ve(lambda e, r=r, h=h: e.tensor_copy(
                out=c.bucketT[0][:, 16 * r + h:16 * r + h + 1], in_=ps[6][0][:, 0:1]),
                reads=[ps[6][1]], writes=[c.bucketT[1]])
        kb.sy(lambda e, h=h: e.dma_start(out=c.drows[h:h + 1, :], in_=c.rbA[0][:]),
              reads=[c.rbA[1]], writes=[c.b_drows])
    c.tap("bucketT", c.bucketT[0][:], [c.bucketT[1]], [128, 256])

    sqbs = [sb(f"qsq{i}", [128, 512], BF16, st) for i in range(2)]
    rowtmp = sb("qnormtmp", [1, T], F32, st)
    for e_ in range(NC):
        for hh in range(2):
            h = 2 * e_ + hh

            def qsq(s, e_=e_):
                q = sqbs[s % 2]
                kb.se(lambda e, e_=e_, s=s, q=q: e.activation(
                    out=q[0][:], in_=c.BIGA[e_][0][:, 512 * s:512 * (s + 1)],
                    func=AF.Square),
                    reads=[c.BIGA[e_][1]], writes=[q[1]])

            def qmm(s, hh=hh):
                q = sqbs[s % 2]
                pq = ps[4 + s % 2]
                kb.te(lambda e, s=s, hh=hh, q=q, pq=pq: e.matmul(
                    pq[0][0:1, :], c.halfones[0][:, hh:hh + 1], q[0][:],
                    start=True, stop=True),
                    reads=[q[1], c.halfones[1]], writes=[pq[1]])
                kb.se(lambda e, s=s, pq=pq: e.activation(
                    out=rowtmp[0][0:1, 512 * s:512 * (s + 1)], in_=pq[0][0:1, :],
                    func=AF.Sqrt),
                    reads=[pq[1]], writes=[rowtmp[1]])

            qsq(0)
            qsq(1)
            for s in range(NSC):
                qmm(s)
                if s + 2 < NSC:
                    qsq(s + 2)
            kb.ve(lambda e, h=h: e.reduce_max(
                c.mrow[0][0:1, h:h + 1], rowtmp[0][:], axis=mybir.AxisListType.X),
                reads=[rowtmp[1]], writes=[c.mrow[1]])
            kb.ve(lambda e, h=h: e.tensor_scalar(
                out=c.msc[0][0:1, h:h + 1], in0=c.mrow[0][0:1, h:h + 1],
                scalar1=MH32, scalar2=None, op0=ALU.mult),
                reads=[c.mrow[1]], writes=[c.msc[1]])
            kb.ve(lambda e, h=h: e.tensor_scalar(
                out=rowtmp[0][:], in0=rowtmp[0][:],
                scalar1=c.msc[0][0:1, h:h + 1], scalar2=1.0, op0=ALU.mult, op1=ALU.add),
                reads=[rowtmp[1], c.msc[1]], writes=[rowtmp[1]])
            kb.sy(lambda e, h=h: e.dma_start(
                out=c.drows[16 + h:17 + h, :], in_=rowtmp[0][:]),
                reads=[rowtmp[1]], writes=[c.b_drows])
    st.close()
    kb.barrier()



def _p3_l0_attn(c, nh=NH):
    kb, sb, ps = c.kb, c.sb, c.ps
    o16 = c.XN
    st = ExitStack()
    browB = [sb(f"brow{i}", [128, T], BF16, st) for i in range(2)]
    mbcB = [sb("mbc0", [128, T], BF16, st)]
    sums64 = sb("sums64", [65, 1024], F32, st)

    cbuf = [sb(f"cbuf{i}", [128, 1024], BF16, st) for i in range(2)]
    abuf = [sb(f"abuf{i}", [128, 1024], F32, st) for i in range(2)]
    wt2 = [sb(f"wt2_{i}", [128, 1024], BF16, st) for i in range(2)]

    def fill_rows(h):
        kb.gp(lambda e, h=h: e.dma_start(
            out=browB[h % 2][0][:],
            in_=c.drows[h:h + 1, :].broadcast_to((128, T))),
            reads=[c.b_drows], writes=[browB[h % 2][1]])

    S1acc = sb("S1acc", [128, 32], F32, st)
    s1bufs = [Buf(f"s1c{i}") for i in range(32)]
    S1c = sb("S1c", [128, 16], F32, st)
    m1cols = sb("m1cols", [128, 16], F32, st)
    m1b = sb("m1b", [128, 16], F32, st)
    m2c = sb("m2c", [128, 16], F32, st)
    m2st = sb("m2st", [16, 128], F32, st)

    fill_rows(0)
    for h in range(nh):
        qh, qb = c.BIGA[h // 2]
        r0 = (h % 2) * 64
        brow = browB[h % 2]
        mbc = mbcB[0]
        if h + 1 < nh:
            fill_rows(h + 1)

        kb.sy(lambda e, h=h: e.dma_start(out=c.rbA[0][:], in_=c.drows[16 + h:17 + h, :]),
              reads=[c.b_drows], writes=[c.rbA[1]])
        for j in range(16):
            kb.te(lambda e, j=j: e.transpose(
                ps[3][0][:, j:j + 1], c.rbA[0][0:1, 128 * j:128 * (j + 1)],
                c.identF[0][0:1, 0:1]),
                reads=[c.rbA[1], c.identF[1]], writes=[ps[3][1]], tr=True)
        kb.ve(lambda e: e.tensor_copy(out=m1cols[0][:], in_=ps[3][0][:, 0:16]),
              reads=[ps[3][1]], writes=[m1cols[1]])
        kb.ve(lambda e: e.tensor_scalar(
            out=m1b[0][:], in0=m1cols[0][:], scalar1=-1.0 / 9.0, scalar2=None,
            op0=ALU.mult), reads=[m1cols[1]], writes=[m1b[1]])

        for g in range(2 * NTT):
            sb_, half = divmod(g, 2)
            bigh = c.bigT[g % 2]
            bb = [ps[2 * (g % 2)][1], ps[2 * (g % 2) + 1][1]]
            for tcl in range(2):
                tc = 2 * half + tcl
                kb.te(lambda e, sb_=sb_, tc=tc, tcl=tcl, bigh=bigh, qh=qh, r0=r0:
                      e.matmul(
                          bigh[:, 512 * tcl:512 * (tcl + 1)],
                          qh[r0:r0 + 64, 128 * sb_:128 * (sb_ + 1)],
                          qh[r0:r0 + 64, 512 * tc:512 * (tc + 1)],
                          start=True, stop=True),
                      reads=[qb], writes=[bb[tcl]])
            s1b = s1bufs[16 * half + sb_]
            kb.se(lambda e, sb_=sb_, half=half, bigh=bigh: e.activation(
                out=bigh[:, 0:1024], in_=bigh[:, 0:1024], func=AF.Exp,
                scale=62.0 / 288.0, bias=m1b[0][:, sb_:sb_ + 1],
                accum_out=S1acc[0][:, 16 * half + sb_:16 * half + sb_ + 1]),
                reads=[m1b[1]], writes=bb + [s1b])
        kb.ve(lambda e: e.tensor_add(S1c[0][:], S1acc[0][:, 0:16], S1acc[0][:, 16:32]),
              reads=s1bufs, writes=[S1c[1]])
        kb.se(lambda e: e.activation(
            out=m2c[0][:], in_=S1c[0][:], func=AF.Ln, scale=9.223372036854776e+18),
            reads=[S1c[1]], writes=[m2c[1]])
        kb.ve(lambda e: e.scalar_tensor_tensor(
            out=m2c[0][:], in0=m2c[0][:], scalar=9.0, in1=m1cols[0][:],
            op0=ALU.mult, op1=ALU.add), reads=[m2c[1], m1cols[1]], writes=[m2c[1]])
        kb.ve(lambda e: e.tensor_scalar(
            out=m2c[0][:], in0=m2c[0][:], scalar1=-393.01445137748897,
            scalar2=None, op0=ALU.add), reads=[m2c[1]], writes=[m2c[1]])
        kb.te(lambda e: e.transpose(
            ps[3][0][0:16, 0:128], m2c[0][:], c.identF[0][:]),
            reads=[m2c[1], c.identF[1]], writes=[ps[3][1]], tr=True)
        kb.ve(lambda e: e.tensor_copy(out=m2st[0][:], in_=ps[3][0][0:16, 0:128]),
              reads=[ps[3][1]], writes=[m2st[1]])
        kb.gp(lambda e: e.dma_start(
            out=c.lnrows[5:6, :].rearrange("p (b x) -> (p b) x", b=16), in_=m2st[0][:]),
            reads=[m2st[1]], writes=[c.b_lnrows])
        kb.sy(lambda e: e.dma_start(out=c.rbA[0][:], in_=c.lnrows[5:6, :]),
              reads=[c.b_lnrows], writes=[c.rbA[1]])
        for s in range(NSC):
            kb.te(lambda e, s=s: e.matmul(
                ps[1][0][:], c.ones1f[0][:], c.rbA[0][0:1, 512 * s:512 * (s + 1)],
                start=True, stop=True), reads=[c.rbA[1], c.ones1f[1]], writes=[ps[1][1]])
            kb.ve(lambda e, s=s, mbc=mbc: e.tensor_copy(
                out=mbc[0][:, 512 * s:512 * (s + 1)], in_=ps[1][0][:]),
                reads=[ps[1][1]], writes=[mbc[1]])

        c.tap(f"mbc_h{h}", mbc[0][:], [mbc[1]], [128, T], BF16)

        def amm(r, S):
            bigh = c.bigT[r % 2]
            for cc in range(2):
                kb.te(lambda e, r=r, S=S, cc=cc, bigh=bigh, qh=qh, r0=r0: e.matmul(
                    bigh[:, 512 * cc:512 * (cc + 1)],
                    qh[r0:r0 + 64, 128 * r:128 * (r + 1)],
                    qh[r0:r0 + 64, 1024 * S + 512 * cc:1024 * S + 512 * (cc + 1)],
                    start=True, stop=True),
                    reads=[qb], writes=[ps[2 * (r % 2) + cc][1]])

        def chain(r, S):
            wt = wt2[r % 2]
            cb = cbuf[r % 2]; ab = abuf[r % 2]
            bigh = c.bigT[r % 2]
            bb = [ps[2 * (r % 2)][1], ps[2 * (r % 2) + 1][1]]
            kb.ve(lambda e, r=r, S=S, cb=cb, h=h, brow=brow: e.tensor_scalar(
                out=cb[0][:], in0=brow[0][:, 1024 * S:1024 * (S + 1)],
                scalar1=c.bucketT[0][:, 16 * r + h:16 * r + h + 1], scalar2=62.0,
                op0=ALU.is_equal, op1=ALU.add),
                reads=[brow[1], c.bucketT[1]], writes=[cb[1]])
            kb.ve(lambda e, cb=cb, bigh=bigh, ab=ab: e.scalar_tensor_tensor(
                out=ab[0][:], in0=cb[0][:], scalar=1.0 / 32.0, in1=bigh[:, 0:1024],
                op0=ALU.mult, op1=ALU.mult),
                reads=[cb[1]] + bb, writes=[ab[1]])
            kb.ve(lambda e, S=S, ab=ab, mbc=mbc: e.tensor_sub(
                ab[0][:], ab[0][:], mbc[0][:, 1024 * S:1024 * (S + 1)]),
                reads=[ab[1], mbc[1]], writes=[ab[1]])
            kb.se(lambda e, ab=ab, wt=wt: e.activation(
                out=wt[0][:], in_=ab[0][:], func=AF.Exp, bias=-0.05),
                reads=[ab[1]], writes=[wt[1]])

        def vmm(r, S):
            wt = wt2[r % 2]
            for cc in range(2):
                kb.te(lambda e, r=r, cc=cc, wt=wt, h=h: e.matmul(
                    ps[4 + cc][0][0:65, :], c.vtok[r][0][:, 65 * h:65 * (h + 1)],
                    wt[0][:, 512 * cc:512 * (cc + 1)],
                    start=(r == 0), stop=(r == NTT - 1)),
                    reads=[wt[1], c.vtok[r][1]], writes=[ps[4 + cc][1]])

        for S in range(2):
            for r in range(NTT):
                amm(r, S)
                if r >= 2:
                    vmm(r - 2, S)
                chain(r, S)
            vmm(NTT - 2, S)
            vmm(NTT - 1, S)
            for cc in range(2):
                col0 = 1024 * S + 512 * cc
                if r0 == 0:
                    kb.se(lambda e, cc=cc, col0=col0, h=h: e.activation(
                        out=o16[h // 2][0][0:64, col0:col0 + 512],
                        in_=ps[4 + cc][0][0:64, :], func=AF.Copy),
                        reads=[ps[4 + cc][1]], writes=[o16[h // 2][1]])
                else:
                    kb.se(lambda e, cc=cc: e.activation(
                        out=wt2[0][0][0:64, 512 * cc:512 * (cc + 1)],
                        in_=ps[4 + cc][0][0:64, :], func=AF.Copy),
                        reads=[ps[4 + cc][1]], writes=[wt2[0][1]])
                kb.ve(lambda e, cc=cc: e.tensor_copy(
                    out=sums64[0][64:65, 512 * cc:512 * (cc + 1)],
                    in_=ps[4 + cc][0][64:65, :]),
                    reads=[ps[4 + cc][1]], writes=[sums64[1]])
            kb.sy(lambda e, h=h, S=S: e.dma_start(
                out=c.drows[32 + h:33 + h, 1024 * S:1024 * (S + 1)],
                in_=sums64[0][64:65, :]),
                reads=[sums64[1]], writes=[c.b_drows])
            if r0 != 0:
                kb.gp(lambda e, h=h, S=S: e.dma_start(
                    out=o16[h // 2][0][64:128, 1024 * S:1024 * (S + 1)],
                    in_=wt2[0][0][0:64, :]),
                    reads=[wt2[0][1]], writes=[o16[h // 2][1]])
    st.close()
    kb.barrier()
    if nh < NH:
        return

    st2 = ExitStack()
    sstack = sb("sstack", [16, T], F32, st2)
    onorm = [sb(f"onorm{i}", [128, 512], F32, st2) for i in range(2)]
    kb.sy(lambda e: e.dma_start(out=sstack[0][:], in_=c.drows[32:48, :]),
          reads=[c.b_drows], writes=[sstack[1]])
    kb.ve(lambda e: e.reciprocal(out=sstack[0][:], in_=sstack[0][:]),
          reads=[sstack[1]], writes=[sstack[1]])

    for h in range(NH):
        r0 = (h % 2) * 64
        kb.gp(lambda e, h=h: e.dma_start(out=c.rbA[0][:], in_=sstack[0][h:h + 1, :]),
              reads=[sstack[1]], writes=[c.rbA[1]])
        for s in range(NSC):
            pr = ps[5 + (s % 2)]
            kb.te(lambda e, s=s, pr=pr: e.matmul(
                pr[0][:], c.ones1f[0][:], c.rbA[0][0:1, 512 * s:512 * (s + 1)],
                start=True, stop=True), reads=[c.rbA[1], c.ones1f[1]], writes=[pr[1]])
            ob = onorm[s % 2]
            kb.ve(lambda e, h=h, s=s, pr=pr, ob=ob, r0=r0: e.tensor_mul(
                ob[0][r0:r0 + 64, :], o16[h // 2][0][r0:r0 + 64, 512 * s:512 * (s + 1)],
                pr[0][r0:r0 + 64, :]),
                reads=[o16[h // 2][1], pr[1]], writes=[ob[1]])
            kb.ve(lambda e, h=h, s=s, ob=ob, r0=r0: e.tensor_add(
                c.X[h // 2][0][r0:r0 + 64, 512 * s:512 * (s + 1)],
                c.X[h // 2][0][r0:r0 + 64, 512 * s:512 * (s + 1)], ob[0][r0:r0 + 64, :]),
                reads=[ob[1], c.X[h // 2][1]], writes=[c.X[h // 2][1]])
    c.tap("x1", c.X[0][0][:], [c.X[0][1]], [128, T])
    st2.close()
    c.vtstack.close()
    kb.barrier()



def _p5_l1_ffn(c):
    kb, sb, ps = c.kb, c.sb, c.ps
    st = ExitStack()
    mfull = sb("mfull", [128, 4096], BF16, st)
    GS = [list(range(12 * g, min(12 * (g + 1), 128))) for g in range((128 + 11) // 12)]
    wg0 = [sb(f"wg0_{o}", [128, 12 * 128], BF16, st) for o in range(8)]
    for o in range(8):
        kb.gp(lambda e, o=o: e.dma_start(
            out=wg0[o][0][:, 0:len(GS[0]) * 128],
            in_=c.wcT[o][:, 128 * GS[0][0]:128 * (GS[0][-1] + 1)]),
            writes=[wg0[o][1]])
    pre = ExitStack()
    srm = sb("srm", [16, 4096], BF16, pre)
    sru = sb("sru", [16, 16384], BF16, pre)
    mstage = sb("mstage", [16, 16384], BF16, pre)
    kb.gp(lambda e: e.dma_start(out=srm[0][:], in_=c.rmTsD), writes=[srm[1]])
    kb.gp(lambda e: e.dma_start(out=sru[0][:], in_=c.ruTsD), writes=[sru[1]])
    kb.ve(lambda e: e.memset(mfull[0][:], 0.0), writes=[mfull[1]])
    for g in range(32):
        pz = ps[g % 2]
        for u in range(8):
            blk = g * 8 + u
            kb.te(lambda e, blk=blk, u=u, pz=pz: e.matmul(
                pz[0][0:16, 64 * u:64 * (u + 1)],
                srm[0][:, 16 * blk:16 * (blk + 1)],
                sru[0][:, 64 * blk:64 * (blk + 1)], start=True, stop=True),
                reads=[srm[1], sru[1]], writes=[pz[1]])
        kb.ve(lambda e, g=g, pz=pz: e.tensor_copy(
            out=mstage[0][0:16, 512 * g:512 * (g + 1)], in_=pz[0][0:16, :]),
            reads=[pz[1]], writes=[mstage[1]])
    for m in range(4):
        for pp_ in range(2):
            s3 = mstage[0][0:16, :].rearrange("p (i c) -> p i c", c=512)
            s3 = s3[:, :, 64 * (2 * m + pp_):64 * (2 * m + pp_) + 64]
            d3 = mfull[0][32 * m + 16 * pp_:32 * m + 16 * (pp_ + 1), :].rearrange(
                "p (i c) -> p i c", c=128)
            d3 = d3[:, :, 64 * pp_:64 * pp_ + 64]
            kb.gp(lambda e, s3=s3, d3=d3: e.dma_start(out=d3, in_=s3),
                  reads=[mstage[1]], writes=[mfull[1]])
    pre.close()
    kb.barrier()

    partial = [sb(f"pf{i}", [128, T], F32, st) for i in range(NC)]
    wg = [wg0, [sb(f"wg1_{o}", [128, 12 * 128], BF16, st) for o in range(8)]]
    hk = [sb(f"hk{k}", [128, 512], BF16, st) for k in range(12)]
    for gi, G in enumerate(GS):
        gb = wg[gi % 2]
        for t4 in range(NSC):
            for ki, j in enumerate(G):
                h_ = j // 32
                jj = j % 32
                m = jj % 4
                ccol = jj // 4 + 8 * h_
                tn = jj // 4
                pz = ps[4 + (ki % 2)]
                kb.te(lambda e, m=m, ccol=ccol, tn=tn, t4=t4, pz=pz: e.matmul(
                    pz[0][:], mfull[0][32 * m:32 * (m + 1), 128 * ccol:128 * (ccol + 1)],
                    c.XN[tn][0][32 * m:32 * (m + 1), 512 * t4:512 * (t4 + 1)],
                    start=True, stop=True, tile_position=(32 * m, 0)),
                    reads=[mfull[1], c.XN[tn][1]], writes=[pz[1]])
                kb.se(lambda e, ki=ki, pz=pz: e.activation(
                    out=hk[ki][0][:], in_=pz[0][:], func=AF.Gelu),
                    reads=[pz[1]], writes=[hk[ki][1]])
            for o in range(8):
                po = ps[o % 4]
                for ki in range(len(G)):
                    kb.te(lambda e, o=o, ki=ki, po=po, gb=gb, n=len(G): e.matmul(
                        po[0][:], gb[o][0][:, 128 * ki:128 * (ki + 1)], hk[ki][0][:],
                        start=(ki == 0), stop=(ki == n - 1)),
                        reads=[gb[o][1], hk[ki][1]], writes=[po[1]])
                if gi == 0:
                    kb.ve(lambda e, o=o, t4=t4, po=po: e.tensor_copy(
                        out=partial[o][0][:, 512 * t4:512 * (t4 + 1)], in_=po[0][:]),
                        reads=[po[1]], writes=[partial[o][1]])
                else:
                    kb.ve(lambda e, o=o, t4=t4, po=po: e.tensor_add(
                        partial[o][0][:, 512 * t4:512 * (t4 + 1)],
                        partial[o][0][:, 512 * t4:512 * (t4 + 1)], po[0][:]),
                        reads=[po[1], partial[o][1]], writes=[partial[o][1]])
            if t4 == NSC - 1 and gi + 1 < len(GS):
                nb = wg[(gi + 1) % 2]
                NG2 = GS[gi + 1]
                for o in range(8):
                    kb.gp(lambda e, o=o, nb=nb, NG2=NG2: e.dma_start(
                        out=nb[o][0][:, 0:len(NG2) * 128],
                        in_=c.wcT[o][:, 128 * NG2[0]:128 * (NG2[-1] + 1)]),
                        writes=[nb[o][1]])
    for o in range(8):
        kb.sy(lambda e, o=o: e.dma_start(
            out=c.ffn1sp[128 * o:128 * (o + 1), :], in_=partial[o][0][:]),
            reads=[partial[o][1]], writes=[c.b_ffn1sp])
    c.tap("ffn1", partial[0][0][:], [partial[0][1]], [128, T])
    st.close()
    kb.barrier()



def _p6_l1_attn(c):
    kb, sb, ps = c.kb, c.sb, c.ps
    st = ExitStack()
    Q1 = c.BIGA
    K1 = [sb(f"K1_{i}", [128, T], BF16, st) for i in range(NC)]
    V1 = [sb(f"V1_{r}", [128, 1024], BF16, st) for r in range(NTT)]
    mneg = sb("mneg", [1, T], BF16, st)
    cmask = sb("cmaskS", [128, 2048], BF16, st)
    pstk = ExitStack()
    w16 = [sb(f"w1_{i}", [128, 1024], BF16, pstk) for i in range(2)]
    vtmp = [sb(f"v1tmp{i}", [128, 512], BF16, pstk) for i in range(2)]
    sqb = sb("sq1", [128, 512], BF16, pstk)
    sqc = sb("sq1b", [128, 512], BF16, pstk)
    kb.gp(lambda e: e.dma_start(out=cmask[0][:], in_=c.cmaskD), writes=[cmask[1]])

    def proj1(wdram, dst, to_v):
        for e_ in range(NC):
            sl = e_ % 2
            kb.gp(lambda e, e_=e_, sl=sl, wdram=wdram: e.dma_start(
                out=w16[sl][0][:], in_=wdram[e_]), writes=[w16[sl][1]])
            for s in range(NSC):
                pb = ps[s]
                for k in range(NC):
                    kb.te(lambda e, s=s, k=k, sl=sl, pb=pb: e.matmul(
                        pb[0][:], w16[sl][0][:, 128 * k:128 * (k + 1)],
                        c.XN[k][0][:, 512 * s:512 * (s + 1)],
                        start=(k == 0), stop=(k == NC - 1)),
                        reads=[w16[sl][1], c.XN[k][1]], writes=[pb[1]])
                if not to_v:
                    kb.ve(lambda e, e_=e_, s=s, pb=pb, dst=dst: e.tensor_copy(
                        out=dst[e_][0][:, 512 * s:512 * (s + 1)], in_=pb[0][:]),
                        reads=[pb[1]], writes=[dst[e_][1]])
                else:
                    vt = vtmp[s % 2]
                    kb.ve(lambda e, pb=pb, vt=vt: e.tensor_copy(
                        out=vt[0][:], in_=pb[0][:]), reads=[pb[1]], writes=[vt[1]])
                    for b_ in range(4):
                        r = 4 * s + b_
                        half = b_ % 2
                        kb.te(lambda e, vt=vt, b_=b_, half=half: e.transpose(
                            c.pT[0][:, 512 * half:512 * half + 128],
                            vt[0][:, 128 * b_:128 * (b_ + 1)], c.identB[0][:]),
                            reads=[vt[1], c.identB[1]], writes=[c.pT[1]], tr=True)
                        kb.ve(lambda e, r=r, e_=e_, half=half: e.tensor_copy(
                            out=V1[r][0][:, 128 * e_:128 * (e_ + 1)],
                            in_=c.pT[0][:, 512 * half:512 * half + 128]),
                            reads=[c.pT[1]], writes=[V1[r][1]])

    proj1(c.wk1, K1, to_v=False)
    proj1(c.wq1, Q1, to_v=False)
    proj1(c.wv1, None, to_v=True)

    def normsq(src, rowdst):
        def nsq(s, e_, src=src):
            q = sqb if e_ % 2 == 0 else sqc
            kb.se(lambda e, e_=e_, s=s, src=src, q=q: e.activation(
                out=q[0][:], in_=src[e_][0][:, 512 * s:512 * (s + 1)],
                func=AF.Square), reads=[src[e_][1]], writes=[q[1]])

        for s in range(NSC):
            nsq(s, 0)
            for e_ in range(NC):
                q = sqb if e_ % 2 == 0 else sqc
                if e_ + 1 < NC:
                    nsq(s, e_ + 1)
                kb.te(lambda e, e_=e_, s=s, q=q: e.matmul(
                    ps[2][0][0:1, :], c.ones128[0][:], q[0][:],
                    start=(e_ == 0), stop=(e_ == NC - 1)),
                    reads=[q[1], c.ones128[1]], writes=[ps[2][1]])
            kb.se(lambda e, s=s, rowdst=rowdst: e.activation(
                out=rowdst[0][0:1, 512 * s:512 * (s + 1)], in_=ps[2][0][0:1, :],
                func=AF.Sqrt), reads=[ps[2][1]], writes=[rowdst[1]])

    normsq(K1, c.rbA)
    kb.ve(lambda e: e.reduce_max(c.sca[0][0:1, 0:1], c.rbA[0][:], axis=mybir.AxisListType.X),
          reads=[c.rbA[1]], writes=[c.sca[1]])
    kb.ve(lambda e: e.tensor_scalar(
        out=c.sca[0][0:1, 1:2], in0=c.sca[0][0:1, 0:1], scalar1=-1.02, scalar2=None,
        op0=ALU.mult), reads=[c.sca[1]], writes=[c.sca[1]])
    normsq(Q1, c.rbA)
    kb.ve(lambda e: e.tensor_scalar(
        out=mneg[0][:], in0=c.rbA[0][:], scalar1=c.sca[0][0:1, 1:2], scalar2=-32.0,
        op0=ALU.mult, op1=ALU.add), reads=[c.rbA[1], c.sca[1]], writes=[mneg[1]])
    pstk.close()

    wt1 = [sb(f"wt1_{r}", [128, T - 128 * r], BF16, st) for r in range(NTT)]
    dbuf = [sb(f"dbuf{i}", [128, 512], F32, st) for i in range(2)]
    seqn = 0
    for r in range(NTT):
        for j in range(r // 4, NSC):
            pa = ps[seqn % 4]
            seqn += 1
            for k in range(NC):
                kb.te(lambda e, r=r, j=j, k=k, pa=pa: e.matmul(
                    pa[0][:], K1[k][0][:, 128 * r:128 * (r + 1)],
                    Q1[k][0][:, 512 * j:512 * (j + 1)],
                    start=(k == 0), stop=False),
                    reads=[K1[k][1], Q1[k][1]], writes=[pa[1]])
            kb.te(lambda e, j=j, pa=pa: e.matmul(
                pa[0][:], c.ones1b[0][:], mneg[0][0:1, 512 * j:512 * (j + 1)],
                start=False, stop=True),
                reads=[mneg[1], c.ones1b[1]], writes=[pa[1]])
            if j == r // 4:
                ko = r % 4
                db = dbuf[seqn % 2]
                kb.ve(lambda e, ko=ko, pa=pa, db=db: e.tensor_add(
                    db[0][:], pa[0][:], cmask[0][:, 512 * ko:512 * (ko + 1)]),
                    reads=[pa[1], cmask[1]], writes=[db[1]])
                kb.se(lambda e, r=r, ko=ko, db=db: e.activation(
                    out=wt1[r][0][:, 0:512 - 128 * ko], in_=db[0][:, 128 * ko:512],
                    func=AF.Exp, scale=1.0 / 32.0, bias=-0.03),
                    reads=[db[1]], writes=[wt1[r][1]])
            else:
                col0 = 512 * j - 128 * r
                kb.se(lambda e, r=r, col0=col0, pa=pa: e.activation(
                    out=wt1[r][0][:, col0:col0 + 512], in_=pa[0][:], func=AF.Exp,
                    scale=1.0 / 32.0, bias=-0.03),
                    reads=[pa[1]], writes=[wt1[r][1]])

    for j in range(NSC):
        for r in range(4 * j + 4):
            lo = max(512 * j, 128 * r)
            kb.te(lambda e, r=r, j=j, lo=lo: e.matmul(
                ps[2][0][0:1, lo - 512 * j:512], c.ones128[0][:],
                wt1[r][0][:, lo - 128 * r:512 * (j + 1) - 128 * r],
                start=(r == 0), stop=(r == 4 * j + 3)),
                reads=[wt1[r][1], c.ones128[1]], writes=[ps[2][1]])
        kb.ve(lambda e, j=j: e.tensor_copy(
            out=c.rbA[0][0:1, 512 * j:512 * (j + 1)], in_=ps[2][0][0:1, :]),
            reads=[ps[2][1]], writes=[c.rbA[1]])
    kb.ve(lambda e: e.reciprocal(out=c.rbA[0][:], in_=c.rbA[0][:]),
          reads=[c.rbA[1]], writes=[c.rbA[1]])
    kb.sy(lambda e: e.dma_start(out=c.lnrows[4:5, :], in_=c.rbA[0][:]),
          reads=[c.rbA[1]], writes=[c.b_lnrows])

    rbc = [sb(f"rbc{i}", [128, 512], F32, st) for i in range(2)]
    xbuf = [sb(f"xb{i}", [128, 512], F32, st) for i in range(2)]
    fbuf2 = [sb(f"fb2{i}", [128, 512], F32, st) for i in range(2)]
    ob = [sb(f"obf{i}", [128, 512], F32, st) for i in range(2)]

    def evac(j, d, po):
        rb = rbc[j % 2]
        xb = xbuf[d % 2]; fb = fbuf2[d % 2]; o_ = ob[d % 2]
        kb.ve(lambda e, po=po, o_=o_, rb=rb: e.tensor_mul(o_[0][:], po[0][:], rb[0][:]),
              reads=[po[1], rb[1]], writes=[o_[1]])
        kb.ve(lambda e, o_=o_, xb=xb: e.tensor_add(o_[0][:], o_[0][:], xb[0][:]),
              reads=[o_[1], xb[1]], writes=[o_[1]])
        kb.ve(lambda e, o_=o_, fb=fb: e.tensor_add(o_[0][:], o_[0][:], fb[0][:]),
              reads=[o_[1], fb[1]], writes=[o_[1]])
        kb.sy(lambda e, d=d, j=j, o_=o_: e.dma_start(
            out=c.outT[128 * d:128 * (d + 1), 512 * j:512 * (j + 1)], in_=o_[0][:]),
            reads=[o_[1]], writes=[c.b_outT])

    def fill_rbc(j):
        kb.sy(lambda e, j=j: e.dma_start(
            out=rbc[j % 2][0][:],
            in_=c.lnrows[4:5, 512 * j:512 * (j + 1)].broadcast_to((128, 512))),
            reads=[c.b_lnrows], writes=[rbc[j % 2][1]])

    pend = None
    kb.te(lambda e: e.matmul(
        ps[2][0][:], c.ones1f[0][:], c.rbA[0][0:1, 0:512],
        start=True, stop=True), reads=[c.rbA[1], c.ones1f[1]], writes=[ps[2][1]])
    kb.ve(lambda e: e.tensor_copy(out=rbc[0][0][:], in_=ps[2][0][:]),
          reads=[ps[2][1]], writes=[rbc[0][1]])
    for j in range(NSC):
        for d in range(NC):
            if d == 1 and j + 1 < NSC:
                fill_rbc(j + 1)
            po = ps[3 + (d % 4)]
            xb = xbuf[d % 2]; fb = fbuf2[d % 2]
            kb.sy(lambda e, d=d, j=j, xb=xb: e.dma_start(
                out=xb[0][:], in_=c.xspill[128 * d:128 * (d + 1), 512 * j:512 * (j + 1)]),
                reads=[c.b_xspill], writes=[xb[1]])
            kb.sy(lambda e, d=d, j=j, fb=fb: e.dma_start(
                out=fb[0][:], in_=c.ffn1sp[128 * d:128 * (d + 1), 512 * j:512 * (j + 1)]),
                reads=[c.b_ffn1sp], writes=[fb[1]])
            for r in range(4 * j + 4):
                lo = max(512 * j, 128 * r)
                kb.te(lambda e, r=r, j=j, d=d, po=po, lo=lo: e.matmul(
                    po[0][:, lo - 512 * j:512], V1[r][0][:, 128 * d:128 * (d + 1)],
                    wt1[r][0][:, lo - 128 * r:512 * (j + 1) - 128 * r],
                    start=(r == 0), stop=(r == 4 * j + 3)),
                    reads=[V1[r][1], wt1[r][1]], writes=[po[1]])
            if pend is not None:
                evac(*pend)
            pend = (j, d, po)
    evac(*pend)
    st.close()



def kernel(**inputs):
    maps = host_layout(inputs)
    nc, _c = build_program()
    res = run_bass_kernel_spmd(nc, maps, core_ids=list(range(NCORES)))
    out = np.stack([np.ascontiguousarray(np.asarray(r["outT"]).T) for r in res.results],
                   axis=0)
    return out.astype(np.float32)
